# version 11
# baseline (speedup 1.0000x reference)
import numpy as np
import ml_dtypes

import concourse.bass as bass
import concourse.mybir as mybir
from concourse.bass import ds
from concourse.bass_utils import run_bass_kernel_spmd
from concourse.expressions import RuntimeValue

F32 = mybir.dt.float32
BF16 = mybir.dt.bfloat16
I32 = mybir.dt.int32
AF = mybir.ActivationFunctionType
ALU = mybir.AluOpType
AX = mybir.AxisListType

R = 8
P = 128
TB = 512
T = 1024
D = 2048
DC = 16
NH = 16
HD = 128
ROT = 32
FF = 1408
FC = 11
NE = 8
NB = 8
EPS = 1e-5
NEGBIG = -1.0e9

ENGINES = ("sync", "gpsimd", "tensor", "vector", "scalar")

DEBUG_TAPS = ()
PHASES = 7
MOE_SERIAL = False
AG2_DELAY = False
_CACHE = {}


class Step:
    __slots__ = ("eng", "fn", "deps", "inc", "val", "tag", "is_async", "self_inc")

    def __init__(self, eng, fn, deps, inc, tag, is_async, self_inc):
        self.eng = eng
        self.fn = fn
        self.deps = [d for d in deps if d is not None]
        self.inc = inc
        self.tag = tag
        self.is_async = is_async
        self.self_inc = self_inc


class Plan:
    def __init__(self):
        self.by_eng = {e: [] for e in ENGINES}

    def add(self, eng, fn, deps=(), inc=1, tag="", is_async=None, self_inc=False):
        if is_async is None:
            is_async = inc >= 16
        st = Step(eng, fn, deps, inc, tag, is_async, self_inc)
        lst = self.by_eng[eng]
        prev = lst[-1].val if lst else 0
        st.val = prev + inc
        lst.append(st)
        return st

    def emit(self, eng_name, eng, sems):
        hiwater = {e: 0 for e in ENGINES}
        for st in self.by_eng[eng_name]:
            need = {}
            for d in st.deps:
                if d.eng == eng_name and not d.is_async:
                    continue
                need[d.eng] = max(need.get(d.eng, 0), d.val)
            for pe, v in sorted(need.items()):
                if v > hiwater[pe]:
                    eng.wait_ge(sems[pe], v)
                    hiwater[pe] = v
            last = st.fn(eng)
            if st.inc and not st.self_inc:
                assert last is not None, f"step {st.tag} returned no instruction"
                last.then_inc(sems[eng_name], st.inc)


def host_layouts(inputs):
    x = np.asarray(inputs["L0_attn_hidden_states"], np.float32)
    w_qkv = np.asarray(inputs["L0_attn_w_qkv"], np.float32)
    b_qkv = np.asarray(inputs["L0_attn_b_qkv"], np.float32)
    w_out = np.asarray(inputs["L0_attn_w_out"], np.float32)
    b_out = np.asarray(inputs["L0_attn_b_out"], np.float32)
    Wg = np.asarray(inputs["L0_ffn_Wg"], np.float32)
    W1 = np.asarray(inputs["L0_ffn_W1"], np.float32)
    W2 = np.asarray(inputs["L0_ffn_W2"], np.float32)
    W3 = np.asarray(inputs["L0_ffn_W3"], np.float32)
    c = np.ascontiguousarray

    wq = w_qkv.reshape(D, NH, 3, HD)
    wqk = c(wq[:, :, 0:2, :].reshape(D, NH * 2 * HD))
    wv = c(wq[:, :, 2, :].reshape(D, NH * HD))
    bq = b_qkv.reshape(NH, 3, HD)
    bqk = c(bq[:, 0:2, :].reshape(NH * 2, HD).T)
    bv = c(bq[:, 2, :].reshape(NH, HD).T)

    wqks = c(wqk.reshape(DC, P, 16, 256).transpose(2, 1, 0, 3).reshape(16, P, DC * 256))
    wvs = c(wv.reshape(DC, P, 8, 256).transpose(2, 1, 0, 3).reshape(8, P, DC * 256))
    wgs = c(Wg.T.reshape(DC, P, NE).transpose(1, 0, 2).reshape(P, DC * NE))
    w1s = c(W1.transpose(0, 2, 1).reshape(NE, DC, P, FC, 128).transpose(0, 3, 2, 1, 4)
            .reshape(NE, FC, P, DC * 128))
    w2s = c(W2.transpose(0, 2, 1).reshape(NE, DC, P, FC, 128).transpose(0, 3, 2, 1, 4)
            .reshape(NE, FC, P, DC * 128))
    w3s = c(W3.transpose(0, 2, 1).reshape(NE, FC, P, 4, 512).transpose(0, 3, 2, 1, 4)
            .reshape(NE, 4, P, FC * 512))
    wos = c(w_out.reshape(DC, P, 4, 512).transpose(2, 1, 0, 3).reshape(4, P, DC * 512))

    inv = 1.0 / (10000.0 ** (np.arange(0, ROT, 2, dtype=np.float64) / ROT))
    pos = np.arange(4096, dtype=np.float64)
    ang = pos[:, None] * inv[None, :]
    emb = np.concatenate([ang, ang], axis=1)
    cos_full = np.cos(emb).astype(np.float32).T
    sin2_full = np.sin(emb).astype(np.float32).T
    sin2_full[0:16, :] *= -1.0

    jj = np.arange(4)[:, None, None]
    pp = np.arange(P)[None, :, None]
    qq = np.arange(512)[None, None, :]
    masks = np.where(jj * P + pp <= qq, np.float32(0.0), np.float32(NEGBIG))
    masks = c(masks.astype(ml_dtypes.bfloat16).transpose(1, 0, 2)
              .reshape(P, 4 * 512).view(np.uint16))

    ident = np.zeros((P, 64), np.float32)
    ident[:, :] = np.eye(P, dtype=ml_dtypes.bfloat16).view(np.float32)

    in_maps = []
    for r in range(R):
        sl = slice(TB * r, TB * (r + 1))
        offs = np.zeros((1, 32), np.int64)
        for hl in range(2):
            for b in range(2):
                hb = hl * 2 + b
                h = 2 * r + hl
                offs[0, hb] = (h * 2 + b) * P * 512
                offs[0, 4 + hb] = (4096 + (h * 2 + b) * P) * 512
                for ts in range(4):
                    offs[0, 8 + hb * 4 + ts] = ((h * 2 + b) * TB + ts * P) * HD
        offs[0, 24] = r * TB
        in_maps.append({
            "x0": c(x[0, sl, :]), "x1": c(x[1, sl, :]),
            "wqk": wqks, "wv": wvs, "bqk": bqk, "bv": bv,
            "bout": b_out.reshape(1, D),
            "wg": wgs, "w1s": w1s, "w2s": w2s, "w3s": w3s, "wos": wos,
            "cos_t": c(cos_full[:, sl]), "sin2_t": c(sin2_full[:, sl]),
            "masks_u16": masks,
            "ident": ident,
            "offs": offs.astype(np.int32),
        })
    return in_maps


def _emit_all(nc, pl, sems, tail_steps):
    pl.add("sync", lambda e: None, deps=list(tail_steps), inc=0, tag="final")

    with nc.Block() as block:
        @block.sync
        def _(e):
            pl.emit("sync", e, sems)

        @block.gpsimd
        def _(e):
            pl.emit("gpsimd", e, sems)

        @block.tensor
        def _(e):
            pl.emit("tensor", e, sems)

        @block.vector
        def _(e):
            pl.emit("vector", e, sems)

        @block.scalar
        def _(e):
            pl.emit("scalar", e, sems)

    nc.finalize()
    return nc


def build_program():
    nc = bass.Bass("TRN2", target_bir_lowering=False)
    pl = Plan()
    A = pl.add

    def din(name, shape, dt=F32):
        return nc.dram_tensor(name, list(shape), dt, kind="ExternalInput").ap()

    x0_d = din("x0", (TB, D))
    x1_d = din("x1", (TB, D))
    wqk_d = din("wqk", (16, P, DC * 256))
    wv_d = din("wv", (8, P, DC * 256))
    bqk_d = din("bqk", (P, 32))
    bv_d = din("bv", (P, 16))
    bout_d = din("bout", (1, D))
    wg_d = din("wg", (P, DC * NE))
    w1_d = din("w1s", (NE, FC, P, DC * 128))
    w2_d = din("w2s", (NE, FC, P, DC * 128))
    w3_d = din("w3s", (NE, 4, P, FC * 512))
    wo_d = din("wos", (4, P, DC * 512))
    cos_d = din("cos_t", (ROT, TB))
    sin_d = din("sin2_t", (ROT, TB))
    mask_d = din("masks_u16", (P, 4 * 512), mybir.dt.uint16)
    id_d = din("ident", (P, 64))
    offs_d = din("offs", (1, 32), I32)

    out_d = nc.dram_tensor("out", [2, TB, D], F32, kind="ExternalOutput").ap()
    taps = {}
    for tname, shape, dt in [
        ("xn", [P, 8 * D], BF16), ("q", [P, NH * T], BF16), ("k", [P, NH * T], BF16),
        ("v", [P, 32 * 4 * HD], BF16), ("w", [P, 8 * NE], F32), ("y", [P, 8 * D], F32),
        ("attn", [P, 4 * NB * 512], BF16), ("xnt", [P, DC * T], BF16),
        ("kv6", [P, 3 * NB * 512], BF16), ("at7", [P, 2 * NH * 512], BF16),
        ("ag7", [P, 4096], BF16),
    ]:
        if tname in DEBUG_TAPS:
            taps[tname] = nc.dram_tensor("t_" + tname, shape, dt, kind="ExternalOutput").ap()

    qk_in = nc.dram_tensor("qk_in", [8192, 512], BF16, kind="Internal")
    qk_out = nc.dram_tensor("qk_out", [65536, 512], BF16, kind="Internal")
    v_in = nc.dram_tensor("v_in", [16384, 128], BF16, kind="Internal")
    v_out = nc.dram_tensor("v_out", [131072, 128], BF16, kind="Internal")
    a2_in = nc.dram_tensor("a2_in", [P, 16384], BF16, kind="Internal")
    a2_out = nc.dram_tensor("a2_out", [1024, 16384], BF16, kind="Internal")
    bar_in = nc.dram_tensor("bar_in", [1, 128], F32, kind="Internal")
    bar_out = nc.dram_tensor("bar_out", [1, 128], F32, kind="Internal")

    KB = 1024
    ar_t = nc.alloc_sbuf_tensor("ar_t", [P, 32 * KB], mybir.dt.uint8)
    ar_y = nc.alloc_sbuf_tensor("ar_y", [P, 64 * KB], mybir.dt.uint8)
    ar_m = nc.alloc_sbuf_tensor("ar_m", [P, 64 * KB], mybir.dt.uint8)
    ar_x = nc.alloc_sbuf_tensor("ar_x", [P, 32 * KB], mybir.dt.uint8)

    def view(arena, off_kb, nbytes_kb, dt, shape):
        apv = arena[:, off_kb * KB:(off_kb + nbytes_kb) * KB].bitcast(dt)
        if len(shape) == 2:
            assert apv.shape[1] == shape[1], (apv.shape, shape)
            return apv
        pat = "p (" + " ".join(f"d{i}" for i in range(1, len(shape))) + ") -> p " + \
              " ".join(f"d{i}" for i in range(1, len(shape)))
        kw = {f"d{i}": shape[i] for i in range(1, len(shape))}
        return apv.rearrange(pat, **kw)

    xnT = view(ar_t, 0, 32, BF16, (P, DC, T))
    at_sb = view(ar_t, 0, 32, BF16, (P, 2, NH, 512))

    xn_tm = view(ar_y, 0, 32, BF16, (P, 8, D))
    Qfm = view(ar_y, 0, 32, BF16, (P, NH, T))
    Kfm = view(ar_y, 32, 32, BF16, (P, NH, T))
    y_acc = view(ar_y, 0, 64, F32, (P, 8, D))

    xst = view(ar_m, 0, 16, F32, (P, 2, D))
    Vsb = view(ar_m, 0, 32, BF16, (P, 32, 4, HD))
    wqkb = view(ar_m, 32, 16, BF16, (P, 2, DC, 256))
    qkscr = view(ar_m, 48, 4, F32, (P, 2, 512))
    vtmp = view(ar_m, 52, 2, BF16, (P, 2, 512))
    rsh = view(ar_m, 54, 2, F32, (ROT, 512))[0:ROT]
    rt1 = view(ar_m, 56, 2, F32, (ROT, 512))[0:ROT]
    rt2 = view(ar_m, 58, 2, F32, (ROT, 512))[0:ROT]
    hwT = view(ar_m, 0, 22, BF16, (P, FC, T))
    w1b = view(ar_m, 22, 8, BF16, (P, 2, DC, 128))
    w2b = view(ar_m, 30, 8, BF16, (P, 2, DC, 128))
    w3b = view(ar_m, 38, 22, BF16, (P, 2, FC, 512))
    sil = view(ar_m, 60, 2, BF16, (P, 2, 512))
    tts = view(ar_m, 62, 2, BF16, (P, 2, 512))
    e4 = view(ar_m, 0, 32, BF16, (P, 16384))
    kbig = view(ar_m, 32, 8, BF16, (P, NB, 512))
    vbig = view(ar_m, 40, 8, BF16, (P, NB, 4, HD))
    qbig = view(ar_m, 48, 8, BF16, (P, NB, 512))
    al_sb = view(ar_m, 56, 1, BF16, (P, 512))
    rb_sb = view(ar_m, 57, 2, F32, (P, 512))
    wob = view(ar_m, 0, 32, BF16, (P, 2, DC, 512))
    gsc = view(ar_m, 32, 4, F32, (P, 2, 512))
    sgs = view(ar_m, 36, 4, F32, (P, 2, 512))
    xrb = view(ar_m, 40, 4, F32, (P, 2, 512))
    osb = view(ar_m, 44, 4, F32, (P, 2, 512))

    attn_sb = view(ar_x, 0, 32, BF16, (P, 4, NB, 512))

    sbt = nc.alloc_sbuf_tensor
    bqk_sb = sbt("bqk_sb", [P, 32], F32)
    bv_sb = sbt("bv_sb", [P, 16], F32)
    bout_b = sbt("bout_b", [1, D], BF16)
    cos_sb = sbt("cos_sb", [ROT, TB], F32)
    sin_sb = sbt("sin_sb", [ROT, TB], F32)
    mask_sb = sbt("mask_sb", [P, 4, 512], BF16)
    identb = sbt("identb", [P, P], BF16)
    onescol = sbt("onescol", [1, P], BF16)
    onesP = sbt("onesP", [P, P], BF16)
    wg_sb = sbt("wg_sb", [P, DC, NE], BF16)
    offs_sb = sbt("offs_sb", [1, 32], I32)
    bnbuf = sbt("bnbuf", [P, 8, 24], F32)
    mv = sbt("mv", [P, 8, 2], F32)
    rstd = sbt("rstd", [P, 8], F32)
    negmrs = sbt("negmrs", [P, 8], F32)
    vareps = sbt("vareps", [P, 8], F32)
    stdv = sbt("stdv", [P, 8], F32)
    rsc = sbt("rsc", [P, 8, NE], F32)
    m1s = sbt("m1s", [P, 8], F32)
    m2s = sbt("m2s", [P, 8], F32)
    d12 = sbt("d12", [P, 8], F32)
    p1s = sbt("p1s", [P, 8], F32)
    p2s = sbt("p2s", [P, 8], F32)
    eqa = sbt("eqa", [P, NE], F32)
    eqb = sbt("eqb", [P, NE], F32)
    eqm = sbt("eqm", [P, NE], F32)
    wful = sbt("wful", [P, 8, NE], F32)

    psA = nc.alloc_psum_tensor("psA", [P, 4, 512], F32)
    ps4a = nc.alloc_psum_tensor("ps4a", [P, 512], F32)
    ps4b = nc.alloc_psum_tensor("ps4b", [P, 512], F32)
    psS = nc.alloc_psum_tensor("psS", [P, 512], F32)
    psTs = [ps4a[:, :].bitcast(BF16)[:, 0:128], ps4b[:, :].bitcast(BF16)[:, 0:128]]

    sems = {e: nc.alloc_semaphore("sem_" + e) for e in ENGINES}

    def MM(e, out, lhsT, rhs, start, stop):
        return e.matmul(out, lhsT, rhs, start=start, stop=stop, skip_group_check=True)

    def ldcfn(e):
        sm = sems["sync"]
        e.dma_start(out=bqk_sb[:, :], in_=bqk_d).then_inc(sm, 16)
        e.dma_start(out=bv_sb[:, :], in_=bv_d).then_inc(sm, 16)
        e.dma_start(out=cos_sb[:, :], in_=cos_d).then_inc(sm, 16)
        e.dma_start(out=sin_sb[:, :], in_=sin_d).then_inc(sm, 16)
        e.dma_start(out=mask_sb[:, :, :],
                    in_=mask_d.bitcast(BF16).rearrange("p (m f) -> p m f", m=4)).then_inc(sm, 16)
        e.dma_start(out=identb[:, :], in_=id_d.bitcast(BF16)).then_inc(sm, 16)
        e.dma_start(out=offs_sb[:, :], in_=offs_d).then_inc(sm, 16)
    ld_c = A("sync", ldcfn, inc=16 * 7, tag="ld_c", self_inc=True)
    ld_wg = A("gpsimd", lambda e: e.dma_start(
        out=wg_sb[:, :, :], in_=wg_d.rearrange("p (dc e2) -> p dc e2", dc=DC)),
        inc=16, tag="ld_wg")
    ld_bo = A("gpsimd", lambda e: e.dma_start(out=bout_b[:, :], in_=bout_d),
              inc=16, tag="ld_bo")
    consts = A("vector", lambda e: (
        e.memset(onescol[:, :], 1.0),
        e.memset(onesP[:, :], 1.0))[-1],
        deps=[], tag="consts")

    x_ld = []
    for tc in range(8):
        xd = x0_d if tc < 4 else x1_d
        ts = tc % 4
        deps = []
        if tc >= 2:
            pass
        x_ld.append(A("sync", (lambda xd, ts, tc: lambda e: e.dma_start(
            out=xst[:, tc % 2, :], in_=xd[ts * 128:(ts + 1) * 128, :]))(xd, ts, tc),
            deps=[], inc=16, tag=f"xld{tc}"))

    ln_st = []
    s_xn = []
    for tc in range(8):
        def stfn(tc=tc):
            def f(e):
                for i in range(4):
                    e.bn_stats(bnbuf[:, tc, 6 * i:6 * i + 6],
                               xst[:, tc % 2, 512 * i:512 * i + 512])
                e.drain()
                e.bn_aggr(mv[:, tc, :], bnbuf[:, tc, :])
                e.drain()
                ins = e.tensor_scalar_add(vareps[:, tc:tc + 1], mv[:, tc, 1:2], float(EPS))
                e.drain()
                return ins
            return f
        ln_st.append(A("vector", stfn(), deps=[x_ld[tc]], tag=f"lnst{tc}"))
        s_sq = A("scalar", (lambda tc: lambda e: e.activation(
            stdv[:, tc:tc + 1], vareps[:, tc:tc + 1], AF.Sqrt))(tc),
            deps=[ln_st[tc]], tag=f"lnsq{tc}")

        def xnfn(tc=tc):
            def f(e):
                e.reciprocal(rstd[:, tc:tc + 1], stdv[:, tc:tc + 1])
                e.drain()
                e.scalar_tensor_tensor(negmrs[:, tc:tc + 1], mv[:, tc, 0:1], -1.0,
                                       rstd[:, tc:tc + 1], ALU.mult, ALU.mult)
                e.drain()
                ins = e.tensor_scalar(xn_tm[:, tc, :], xst[:, tc % 2, :],
                                      rstd[:, tc:tc + 1], negmrs[:, tc:tc + 1],
                                      ALU.mult, ALU.add)
                e.drain()
                return ins
            return f
        s_xn.append(A("vector", xnfn(), deps=[s_sq, x_ld[tc]], tag=f"xn{tc}"))

    for tc in range(2, 8):
        x_ld[tc].deps.append(ln_st[tc - 2])
        x_ld[tc].deps.append(s_xn[tc - 2])

    if "xn" in DEBUG_TAPS:
        A("sync", lambda e: e.dma_start(out=taps["xn"],
                                        in_=xn_tm[:, :, :].rearrange("p a b -> p (a b)")),
          deps=s_xn, inc=16, tag="tap_xn")

    if PHASES <= 1:
        return _emit_all(nc, pl, sems, s_xn)

    tr_mm, tr_ev = [], []
    for i in range(128):
        tc, dc = i // DC, i % DC
        slot = i % 2
        deps = [s_xn[tc]]
        if i >= 2:
            deps.append(tr_ev[i - 2])
        tr_mm.append(A("tensor", (lambda tc, dc, slot: lambda e: e.transpose(
            psTs[slot], xn_tm[:, tc, dc * 128:dc * 128 + 128], identb[:, :]))(tc, dc, slot),
            deps=deps, tag=f"trm{i}"))
        tr_ev.append(A("vector", (lambda tc, dc, slot: lambda e: e.tensor_copy(
            xnT[:, dc, tc * 128:tc * 128 + 128], psTs[slot]))(tc, dc, slot),
            deps=[tr_mm[i]], tag=f"tre{i}"))

    if "xnt" in DEBUG_TAPS:
        tp = A("sync", lambda e: e.dma_start(
            out=taps["xnt"], in_=xnT[:, :, :].rearrange("p a b -> p (a b)")),
            deps=[tr_ev[-1]], inc=16, tag="tap_xnt")
    if PHASES <= 2:
        return _emit_all(nc, pl, sems, [tr_ev[-1]])

    wq_ld, qkv_mm = [], []
    qk_evac, rope_done, v_evac = {}, {}, {}
    vtr_mm, vtr_ev = [], []
    tile_free = {}

    for gi in range(24):
        wsrc = wqk_d[gi] if gi < 16 else wv_d[gi - 16]
        wdeps = [s_xn[-1]] if gi < 2 else [qkv_mm[(gi - 2) * 4 + 3]]
        wq_ld.append(A("gpsimd", (lambda wsrc, gi: lambda e: e.dma_start(
            out=wqkb[:, gi % 2, :, :],
            in_=wsrc.rearrange("p (dc f) -> p dc f", dc=DC)))(wsrc, gi),
            deps=wdeps, inc=16, tag=f"wq{gi}"))
        for ci in range(2):
            for bp in range(2):
                idx = gi * 4 + ci * 2 + bp
                slot = ci * 2 + bp
                deps = [wq_ld[gi]]
                if idx == 0:
                    deps.append(tr_ev[-1])
                if idx - 4 >= 0:
                    deps.append(tile_free[idx - 4])

                def mmfn(gi=gi, ci=ci, bp=bp, slot=slot):
                    def f(e):
                        ins = None
                        for dc in range(DC):
                            ins = MM(e, psA[:, slot, :],
                                     wqkb[:, gi % 2, dc, ci * 128:ci * 128 + 128],
                                     xnT[:, dc, bp * 512:bp * 512 + 512],
                                     dc == 0, dc == DC - 1)
                        return ins
                    return f
                st_mm = A("tensor", mmfn(), deps=deps, tag=f"qm{idx}")
                qkv_mm.append(st_mm)

                if gi < 16:
                    cch = gi * 2 + ci
                    h, isk = cch // 2, cch % 2
                    dst = Kfm if isk else Qfm
                    evdeps = [st_mm, ld_c, tr_ev[-1]]
                    if idx - 2 >= 0 and (idx - 2) in rope_done:
                        evdeps.append(rope_done[idx - 2])
                        evdeps.append(qk_evac[idx - 2][1])

                    def evacfn(cch=cch, bp=bp, slot=slot):
                        def f(e):
                            ins = e.tensor_scalar(qkscr[:, bp, :], psA[:, slot, :],
                                                  bqk_sb[:, cch:cch + 1], None, ALU.add)
                            e.drain()
                            return ins
                        return f
                    st_ev = A("vector", evacfn(), deps=evdeps, tag=f"qe{idx}")

                    def pcastfn(bp=bp, dst=dst, h=h):
                        def f(e):
                            e.tensor_copy(dst[32:64, h, bp * 512:bp * 512 + 512],
                                          qkscr[32:64, bp, :])
                            ins = e.tensor_copy(dst[64:128, h, bp * 512:bp * 512 + 512],
                                                qkscr[64:128, bp, :])
                            e.drain()
                            return ins
                        return f
                    st_pc = A("vector", pcastfn(), deps=[st_ev], tag=f"qp{idx}")
                    qk_evac[idx] = (st_ev, st_pc)

                    def ropefn(bp=bp, h=h, dst=dst):
                        def f(e):
                            e.stream_shuffle(rsh[:, :], qkscr[0:ROT, bp, :],
                                             [(i + 16) % 32 for i in range(ROT)])
                            e.tensor_tensor(rt1[:, :], qkscr[0:ROT, bp, :], cos_sb[:, :], ALU.mult)
                            e.tensor_tensor(rt2[:, :], rsh[:, :], sin_sb[:, :], ALU.mult)
                            e.drain()
                            ins = e.tensor_tensor(dst[0:ROT, h, bp * 512:bp * 512 + 512],
                                                  rt1[:, :], rt2[:, :], ALU.add)
                            e.drain()
                            return ins
                        return f
                    st_rp = A("vector", ropefn(), deps=[st_ev], tag=f"qr{idx}")
                    rope_done[idx] = st_rp
                    tile_free[idx] = st_rp
                else:
                    hv = (gi - 16) * 2 + ci
                    vdeps = [st_mm, ld_c]
                    if idx - 2 >= 0 and (idx - 2) in v_evac:
                        vdeps.append(tile_free[idx - 2])

                    def vefn(hv=hv, bp=bp, slot=slot):
                        def f(e):
                            ins = e.tensor_scalar(vtmp[:, bp, :], psA[:, slot, :],
                                                  bv_sb[:, hv:hv + 1], None, ALU.add)
                            e.drain()
                            return ins
                        return f
                    st_ve = A("vector", vefn(), deps=vdeps, tag=f"ve{idx}")
                    v_evac[idx] = st_ve
                    last = None
                    for ts in range(4):
                        k = len(vtr_mm)
                        tslot = k % 2
                        dv = [st_ve]
                        if k >= 2:
                            dv.append(vtr_ev[k - 2])
                        else:
                            dv.append(tr_ev[-1])
                        vtr_mm.append(A("tensor", (lambda bp, ts, tslot: lambda e: e.transpose(
                            psTs[tslot],
                            vtmp[:, bp, ts * 128:ts * 128 + 128], identb[:, :]))(bp, ts, tslot),
                            deps=dv, tag=f"vt{k}"))
                        vtr_ev.append(A("vector", (lambda hv, bp, ts, tslot: lambda e: e.tensor_copy(
                            Vsb[:, hv * 2 + bp, ts, :], psTs[tslot]))(hv, bp, ts, tslot),
                            deps=[vtr_mm[-1]], tag=f"vv{k}"))
                        last = vtr_ev[-1]
                    tile_free[idx] = last

    rt_mm, rt_ev = [], []
    for tc in range(8):
        deps = [tr_ev[-1], ld_wg]
        if tc >= 1:
            deps.append(rt_ev[tc - 1])

        def rfn(tc=tc):
            def f(e):
                ins = None
                for dc in range(DC):
                    ins = MM(e, psS[:, 0:8], xnT[:, dc, tc * 128:tc * 128 + 128],
                             wg_sb[:, dc, :], dc == 0, dc == DC - 1)
                return ins
            return f
        rt_mm.append(A("tensor", rfn(), deps=deps, tag=f"rm{tc}"))
        rt_ev.append(A("vector", (lambda tc: lambda e: e.tensor_copy(
            rsc[:, tc, :], psS[:, 0:8]))(tc), deps=[rt_mm[tc]], tag=f"re{tc}"))

    def rt_a(e):
        for tc in range(8):
            e.tensor_reduce(m1s[:, tc:tc + 1], rsc[:, tc, :], AX.X, ALU.max)
        e.drain()
        for tc in range(8):
            e.tensor_scalar(eqa[:, :], rsc[:, tc, :], m1s[:, tc:tc + 1], None, ALU.is_equal)
            e.drain()
            e.scalar_tensor_tensor(eqm[:, :], eqa[:, :], NEGBIG, rsc[:, tc, :], ALU.mult, ALU.add)
            e.drain()
            e.tensor_reduce(m2s[:, tc:tc + 1], eqm[:, :], AX.X, ALU.max)
            e.drain()
        ins = e.tensor_sub(d12[:, :], m1s[:, :], m2s[:, :])
        e.drain()
        return ins
    s_t2a = A("vector", rt_a, deps=[rt_ev[-1]], tag="rta")
    s_sig = A("scalar", lambda e: e.activation(p1s[:, :], d12[:, :], AF.Sigmoid),
              deps=[s_t2a], tag="rsg")

    def rt_b(e):
        e.tensor_scalar(p2s[:, :], p1s[:, :], -1.0, 1.0, ALU.mult, ALU.add)
        e.drain()
        for tc in range(8):
            e.tensor_scalar(eqa[:, :], rsc[:, tc, :], m1s[:, tc:tc + 1], None, ALU.is_equal)
            e.tensor_scalar(eqb[:, :], rsc[:, tc, :], m2s[:, tc:tc + 1], None, ALU.is_equal)
            e.drain()
            e.tensor_scalar(eqm[:, :], eqa[:, :], p1s[:, tc:tc + 1], None, ALU.mult)
            e.tensor_scalar(eqb[:, :], eqb[:, :], p2s[:, tc:tc + 1], None, ALU.mult)
            e.drain()
            ins = e.tensor_add(wful[:, tc, :], eqm[:, :], eqb[:, :])
            e.drain()
        return ins
    s_t2b = A("vector", rt_b, deps=[s_sig], tag="rtb")
    if "w" in DEBUG_TAPS:
        A("sync", lambda e: e.dma_start(out=taps["w"],
                                        in_=wful[:, :, :].rearrange("p a b -> p (a b)")),
          deps=[s_t2b], inc=16, tag="tap_w")

    if PHASES <= 3:
        return _emit_all(nc, pl, sems, [rope_done[max(rope_done)], qk_evac[max(qk_evac)][1],
                                        vtr_ev[-1], moe_yac_dummy_t2b := s_t2b])

    all_rope_last = rope_done[max(rope_done)]
    st_q = A("sync", lambda e: e.dma_start(
        out=qk_in.ap()[0:4096, :].rearrange("(g p) c -> p g c", p=P),
        in_=Qfm[:, :, :].rearrange("p h (b c) -> p (h b) c", b=2)),
        deps=[all_rope_last, qk_evac[max(qk_evac)][1]], inc=16, tag="st_q")
    st_k = A("sync", lambda e: e.dma_start(
        out=qk_in.ap()[4096:8192, :].rearrange("(g p) c -> p g c", p=P),
        in_=Kfm[:, :, :].rearrange("p h (b c) -> p (h b) c", b=2)),
        deps=[], inc=16, tag="st_k")
    st_v = A("sync", lambda e: e.dma_start(
        out=v_in.ap().rearrange("(g ts p) hd -> p g ts hd", p=P, ts=4),
        in_=Vsb[:, :, :, :]),
        deps=[vtr_ev[-1]], inc=16, tag="st_v")
    if "q" in DEBUG_TAPS:
        def tqkfn(e):
            e.dma_start(out=taps["q"],
                        in_=Qfm[:, :, :].rearrange("p a b -> p (a b)")).then_inc(sems["sync"], 16)
            e.dma_start(out=taps["k"],
                        in_=Kfm[:, :, :].rearrange("p a b -> p (a b)")).then_inc(sems["sync"], 16)
        A("sync", tqkfn, deps=[st_q], inc=32, tag="tap_qk", self_inc=True)
    if "v" in DEBUG_TAPS:
        A("sync", lambda e: e.dma_start(
            out=taps["v"], in_=Vsb[:, :, :, :].rearrange("p a b c -> p (a b c)")),
            deps=[st_v], inc=16, tag="tap_v")

    ag1 = A("gpsimd", lambda e: e.collective_compute(
        "AllGather", ALU.bypass, replica_groups=[list(range(R))],
        ins=[qk_in.ap().opt()], outs=[qk_out.ap().opt()]),
        deps=[st_q, st_k], is_async=True, tag="ag1")
    ag1v = A("gpsimd", lambda e: e.collective_compute(
        "AllGather", ALU.bypass, replica_groups=[list(range(R))],
        ins=[v_in.ap().opt()], outs=[v_out.ap().opt()]),
        deps=[st_v], is_async=True, tag="ag1v")

    if PHASES <= 4:
        return _emit_all(nc, pl, sems, [ag1, ag1v, s_t2b])

    moe_w12, moe_w12a, moe_hmm, moe_sil, moe_hw = [], [], [], [], []
    moe_w3, moe_ymm, moe_yac = [], [], []
    _prev = [None]
    _realA = A

    def A(eng, fn, deps=(), tag="", **kw):
        mode = MOE_SERIAL
        chain = (mode == "all") or (mode == "y" and (tag[:2] in ("my", "ya", "m3"))) or                 (mode == "h" and (tag[:2] in ("mw", "mh", "ms")))
        if chain and _prev[0] is not None:
            deps = list(deps) + [_prev[0]]
        st = _realA(eng, fn, deps=deps, tag=tag, **kw)
        if chain or mode == "all":
            _prev[0] = st
        return st
    moe_subs = []
    prev_ld = [None]
    for e_ in range(NE):
        for hc in range(FC):
            i = e_ * FC + hc
            dld = [st_v, qk_evac[max(qk_evac)][1]]
            d1 = dld + ([moe_subs[i - 2][1]] if i >= 2 else [])
            d2 = dld + ([moe_subs[i - 2][3]] if i >= 2 else [])
            if prev_ld[0] is not None:
                d1 = d1 + [prev_ld[0]]

            def w1fn(e_=e_, hc=hc, i=i):
                def f(e):
                    return e.dma_start(out=w1b[:, i % 2, :, :],
                                       in_=w1_d[e_, hc].rearrange("p (dc f) -> p dc f", dc=DC))
                return f

            def w2fn(e_=e_, hc=hc, i=i):
                def f(e):
                    return e.dma_start(out=w2b[:, i % 2, :, :],
                                       in_=w2_d[e_, hc].rearrange("p (dc f) -> p dc f", dc=DC))
                return f
            l1 = A("gpsimd", w1fn(), deps=d1, inc=16, tag=f"mw{i}a")
            l2 = A("gpsimd", w2fn(), deps=d2 + [l1], inc=16, tag=f"mw{i}b")
            moe_w12.append(l2)
            moe_w12a.append(l1)
            prev_ld[0] = l2

            dm0 = []
            if hc == 0 and e_ >= 1:
                dm0.append(moe_ymm[-1])
            subs = []
            for w_i in range(2):
                for tb2 in range(2):
                    slot = w_i * 2 + tb2
                    dms = list(dm0) + [moe_w12a[i] if w_i == 0 else moe_w12[i]]
                    if i >= 1:
                        dms.append(sil_steps_prev[tb2] if w_i == 0 else hw_steps_prev[tb2])
                    else:
                        dms.append(v_evac[max(v_evac)])

                    def hsub(i=i, w_i=w_i, tb2=tb2, slot=slot):
                        def f(e):
                            ins = None
                            wb = w1b if w_i == 0 else w2b
                            for dc in range(DC):
                                ins = MM(e, psA[:, slot, :], wb[:, i % 2, dc, :],
                                         xnT[:, dc, tb2 * 512:tb2 * 512 + 512],
                                         dc == 0, dc == DC - 1)
                            return ins
                        return f
                    subs.append(A("tensor", hsub(), deps=dms, tag=f"mh{i}_{slot}"))
            moe_hmm.append(subs[-1])
            moe_subs.append(subs)

            sil_steps = [
                A("scalar", (lambda tb2: lambda e: e.activation(
                    sil[:, tb2, :], psA[:, tb2, :], AF.Silu))(tb2),
                    deps=[subs[tb2]] + ([hw_steps_prev[tb2]] if i >= 1 else []),
                    tag=f"ms{i}_{tb2}")
                for tb2 in range(2)]
            moe_sil.append(sil_steps[-1])

            hw_steps = []
            for tb2 in range(2):
                dhw = [sil_steps[tb2], subs[2 + tb2], st_v]
                if e_ >= 1 and moe_yac:
                    dhw.append(moe_yac[(e_ - 1) * 32 + 31])

                def hwfn(hc=hc, tb2=tb2):
                    def f(e):
                        ins = e.tensor_tensor(hwT[:, hc, tb2 * 512:tb2 * 512 + 512],
                                              sil[:, tb2, :], psA[:, 2 + tb2, :], ALU.mult)
                        e.drain()
                        return ins
                    return f
                hw_steps.append(A("vector", hwfn(), deps=dhw, tag=f"hv{i}_{tb2}"))
            moe_hw.append(hw_steps[-1])
            sil_steps_prev = sil_steps
            hw_steps_prev = hw_steps

        for db in range(4):
            j = e_ * 4 + db
            dw3 = [st_v, qk_evac[max(qk_evac)][1]]
            if j >= 2:
                dw3.append(moe_ymm[(j - 2) * 8 + 7])
            if prev_ld[0] is not None:
                dw3.append(prev_ld[0])
            moe_w3.append(A("gpsimd", (lambda e_, db, j: lambda e: e.dma_start(
                out=w3b[:, j % 2, :, :],
                in_=w3_d[e_, db].rearrange("p (hc f) -> p hc f", hc=FC)))(e_, db, j),
                deps=dw3, inc=16, tag=f"m3{j}"))
            prev_ld[0] = moe_w3[-1]
            for tc in range(8):
                k = j * 8 + tc
                dy = [moe_w3[j], moe_hw[e_ * FC + FC - 1]]
                if k >= 2:
                    dy.append(moe_yac[k - 2])
                if k == 0:
                    dy.append(st_q)
                    dy.append(st_k)

                def ymmfn(j=j, tc=tc, k=k):
                    def f(e):
                        ins = None
                        pst = ps4a if k % 2 == 0 else ps4b
                        for hc in range(FC):
                            ins = MM(e, pst[:, :], hwT[:, hc, tc * 128:tc * 128 + 128],
                                     w3b[:, j % 2, hc, :], hc == 0, hc == FC - 1)
                        return ins
                    return f
                moe_ymm.append(A("tensor", ymmfn(), deps=dy, tag=f"my{k}"))

                def yacfn(e_=e_, db=db, tc=tc, k=k):
                    def f(e):
                        pst = (ps4a if k % 2 == 0 else ps4b)[:, :]
                        if e_ == 0:
                            ins = e.tensor_scalar(y_acc[:, tc, db * 512:db * 512 + 512], pst,
                                                  wful[:, tc, 0:1], None, ALU.mult)
                        else:
                            ins = e.scalar_tensor_tensor(
                                y_acc[:, tc, db * 512:db * 512 + 512], pst,
                                wful[:, tc, e_:e_ + 1],
                                y_acc[:, tc, db * 512:db * 512 + 512], ALU.mult, ALU.add)
                        e.drain()
                        return ins
                    return f
                moe_yac.append(A("vector", yacfn(), deps=[moe_ymm[k], s_t2b], tag=f"ya{k}"))

    if "y" in DEBUG_TAPS:
        A("sync", lambda e: e.dma_start(out=taps["y"],
                                        in_=y_acc[:, :, :].rearrange("p a b -> p (a b)")),
          deps=[moe_yac[-1]], inc=16, tag="tap_y")

    if PHASES <= 5:
        return _emit_all(nc, pl, sems, [moe_yac[-1], ag1, ag1v])

    A = _realA
    ag1b = A("gpsimd", lambda e: e.collective_compute(
        "AllReduce", ALU.add, replica_groups=[list(range(R))],
        ins=[bar_in.ap().opt()], outs=[bar_out.ap().opt()]),
        deps=[ag1v], is_async=True, tag="ag1bar")

    s_regs = {}

    def mkregs(e):
        regs = [e.alloc_register(f"ar{i}") for i in range(24)]
        for i in range(24):
            e.reg_load(regs[i], offs_sb[0:1, i:i + 1])
        mx = [8064 * 512] * 8 + [16256 * HD] * 16
        s_regs["v"] = [RuntimeValue(r, min_val=0, max_val=mx[i])
                       for i, r in enumerate(regs)]
        return None
    st_regs = A("sync", mkregs, deps=[ld_c, ag1b], inc=0, tag="aregs")

    kld, vld, qld = {}, {}, {}
    amm, aexp, as4, apb, asg = {}, {}, {}, {}, {}
    aal, arb, aev = {}, {}, {}
    amsk = {}
    qk3 = qk_out.ap().rearrange("(blk q) c -> blk (q c)", q=8192)
    v3 = v_out.ap().rearrange("(blk t) hd -> blk (t hd)", t=16384)

    for hb in range(4):
        dld = [st_regs]
        if hb >= 1:
            dld.append(asg[(hb - 1, NB - 1, NB - 1)])
        else:
            dld.append(moe_yac[-1])
            dld.append(moe_hmm[-1])
        qld[hb] = A("sync", (lambda hb: lambda e: e.dma_start(
            out=qbig[:, :, :],
            in_=qk3[0:NB, ds(s_regs["v"][hb], P * 512)]
            .rearrange("blk (p c) -> p blk c", p=P)))(hb),
            deps=dld, inc=16, tag=f"ql{hb}")
        kld[hb] = A("sync", (lambda hb: lambda e: e.dma_start(
            out=kbig[:, :, :],
            in_=qk3[0:NB, ds(s_regs["v"][4 + hb], P * 512)]
            .rearrange("blk (p c) -> p blk c", p=P)))(hb),
            deps=dld, inc=16, tag=f"kl{hb}")

        def vldfn(hb=hb):
            def f(e):
                for ts in range(4):
                    idx = 8 + hb * 4 + ts
                    e.dma_start(
                        out=vbig[:, :, ts, :],
                        in_=v3[0:NB, ds(s_regs["v"][idx], P * HD)]
                        .rearrange("blk (p hd) -> p blk hd", p=P)).then_inc(sems["sync"], 16)
            return f
        vld[hb] = A("sync", vldfn(), deps=dld, inc=16 * 4, tag=f"vl{hb}", self_inc=True)

        for qc in range(NB):
            key = (hb, qc)
            for kk in range(qc + 1):
                kkey = (hb, qc, kk)
                dm = [kld[hb], qld[hb]]
                if kk >= 1:
                    dm.append(aexp[(hb, qc, kk - 1)][1])
                elif qc >= 1:
                    dm.append(aexp[(hb, qc - 1, qc - 1)][1])
                elif hb >= 1:
                    dm.append(aexp[(hb - 1, NB - 1, NB - 1)][1])
                else:
                    dm.append(moe_yac[-1])
                    dm.append(moe_sil[-1])
                    dm.append(moe_hw[-1])
                if kk == 0:
                    if qc >= 1:
                        dm.append(aal[(hb, qc - 1)])
                    elif hb >= 1:
                        dm.append(aal[(hb - 1, NB - 1)])

                def ammfn(qc=qc, kk=kk, half=0):
                    def f(e):
                        ins = None
                        for jj2 in (0, 1) if half == 0 else (2, 3):
                            ins = MM(e, psA[:, jj2, :],
                                     kbig[:, kk, jj2 * 128:jj2 * 128 + 128],
                                     qbig[:, qc, :], True, True)
                        return ins
                    return f
                dmA = list(dm)
                dmB = list(dm)
                if kk >= 1:
                    dmB[2] = aexp[(hb, qc, kk - 1)][1]
                    dmA[2] = aexp[(hb, qc, kk - 1)][0]
                ammA = A("tensor", ammfn(qc, kk, 0), deps=dmA, tag=f"am{hb}{qc}{kk}a")
                ammB = A("tensor", ammfn(qc, kk, 1), deps=dmB, tag=f"am{hb}{qc}{kk}b")
                amm[kkey] = ammB

                if kk == qc:
                    def mskfn():
                        def f(e):
                            ins = e.tensor_tensor(psA[:, :, :], psA[:, :, :],
                                                  mask_sb[:, :, :], ALU.add)
                            e.drain()
                            return ins
                        return f
                    amsk[key] = A("vector", mskfn(), deps=[ammB, ld_c], tag=f"ak{hb}{qc}")
                    edepA = [amsk[key]]
                    edepB = [amsk[key]]
                else:
                    edepA = [ammA]
                    edepB = [ammB]

                expA = A("scalar", (lambda kk: lambda e: e.activation(
                    e4[:, kk * 2048:kk * 2048 + 1024],
                    psA[:, 0:2, :].rearrange("p a b -> p (a b)"), AF.Exp,
                    bias=0.0, scale=0.5))(kk),
                    deps=edepA, tag=f"ae{hb}{qc}{kk}a")
                expB = A("scalar", (lambda kk: lambda e: e.activation(
                    e4[:, kk * 2048 + 1024:kk * 2048 + 2048],
                    psA[:, 2:4, :].rearrange("p a b -> p (a b)"), AF.Exp,
                    bias=0.0, scale=0.5))(kk),
                    deps=edepB, tag=f"ae{hb}{qc}{kk}b")
                aexp[kkey] = (expA, expB)

                def s4fn(kk=kk, qc=qc):
                    def f(e):
                        ins = None
                        for jj2 in range(4):
                            ins = MM(e, ps4a[:, :], onesP[:, :],
                                     e4[:, (kk * 4 + jj2) * 512:(kk * 4 + jj2 + 1) * 512],
                                     kk == 0 and jj2 == 0, kk == qc and jj2 == 3)
                        return ins
                    return f
                ds4 = [aexp[kkey][1]]
                if kk == 0:
                    if qc >= 1:
                        ds4.append(aal[(hb, qc - 1)])
                    elif hb >= 1:
                        ds4.append(aal[(hb - 1, NB - 1)])
                    else:
                        ds4.append(moe_yac[-1])
                as4[kkey] = A("tensor", s4fn(), deps=ds4, tag=f"a4{hb}{qc}{kk}")

            def aalfn():
                def f(e):
                    with e.bass.allow_low_precision("alpha cancels in final normalize"):
                        ins = e.reciprocal(al_sb[:, :], ps4a[:, :])
                    return ins
                return f
            aal[key] = A("vector", aalfn(), deps=[as4[(hb, qc, qc)]], tag=f"aa{hb}{qc}")

            for kk in range(qc + 1):
                kkey = (hb, qc, kk)

                def pbmul(kk=kk):
                    def f(e):
                        sl3 = e4[:, kk * 2048:(kk + 1) * 2048].rearrange(
                            "p (k f) -> p k f", k=4)
                        a4 = al_sb[:, :].rearrange("p (o f) -> p o f", o=1).broadcast_to(
                            [P, 4, 512])
                        ins = e.tensor_tensor(sl3, sl3, a4, ALU.mult)
                        e.drain()
                        return ins
                    return f

                def pbsq(kk=kk):
                    def f(e):
                        sl2 = e4[:, kk * 2048:(kk + 1) * 2048]
                        ins = e.tensor_tensor(sl2, sl2, sl2, ALU.mult)
                        e.drain()
                        return ins
                    return f
                pm = A("vector", pbmul(), deps=[aal[key], aexp[kkey][1]],
                       tag=f"ab{hb}{qc}{kk}")
                if kk % 2 == 0:
                    apb[kkey] = A("scalar", (lambda kk: lambda e: e.activation(
                        e4[:, kk * 2048:(kk + 1) * 2048],
                        e4[:, kk * 2048:(kk + 1) * 2048], AF.Square))(kk),
                        deps=[pm], tag=f"as{hb}{qc}{kk}")
                else:
                    apb[kkey] = A("vector", pbsq(), deps=[pm], tag=f"ab2{hb}{qc}{kk}")

                def sgfn(kk=kk, qc=qc):
                    def f(e):
                        ins = None
                        for jj2 in range(4):
                            MM(e, ps4b[:, :], onesP[:, :],
                               e4[:, (kk * 4 + jj2) * 512:(kk * 4 + jj2 + 1) * 512],
                               kk == 0 and jj2 == 0, kk == qc and jj2 == 3)
                            ins = MM(e, psS[:, :], vbig[:, kk, jj2, :],
                                     e4[:, (kk * 4 + jj2) * 512:(kk * 4 + jj2 + 1) * 512],
                                     kk == 0 and jj2 == 0, kk == qc and jj2 == 3)
                        return ins
                    return f
                dsg = [apb[kkey], vld[hb]]
                if kk == 0:
                    if qc >= 1:
                        dsg.append(aev[(hb, qc - 1)])
                    elif hb >= 1:
                        dsg.append(aev[(hb - 1, NB - 1)])
                    else:
                        dsg.append(moe_yac[-1])
                asg[kkey] = A("tensor", sgfn(), deps=dsg, tag=f"ag{hb}{qc}{kk}")

            arb[key] = A("vector", (lambda: lambda e: e.reciprocal(rb_sb[:, :], ps4b[:, :]))(),
                         deps=[asg[(hb, qc, qc)]], tag=f"ar{hb}{qc}")

            def aevfn(hb=hb, qc=qc):
                def f(e):
                    e.drain()
                    ins = e.tensor_tensor(attn_sb[:, hb, qc, :], psS[:, :], rb_sb[:, :], ALU.mult)
                    e.drain()
                    return ins
                return f
            aev[key] = A("vector", aevfn(), deps=[arb[key]], tag=f"av{hb}{qc}")

    if "kv6" in DEBUG_TAPS:
        def kv6fn(e):
            e.dma_start(out=taps["kv6"][:, 0:NB * 512],
                        in_=kbig[:, :, :].rearrange("p a b -> p (a b)")).then_inc(sems["sync"], 16)
            e.dma_start(out=taps["kv6"][:, NB * 512:2 * NB * 512],
                        in_=qbig[:, :, :].rearrange("p a b -> p (a b)")).then_inc(sems["sync"], 16)
            e.dma_start(out=taps["kv6"][:, 2 * NB * 512:3 * NB * 512],
                        in_=vbig[:, :, :, :].rearrange("p a b c -> p (a b c)")).then_inc(sems["sync"], 16)
        A("sync", kv6fn, deps=[asg[(0, NB - 1, NB - 1)]], inc=48, tag="tap_kv6",
          self_inc=True)

    if "attn" in DEBUG_TAPS:
        A("sync", lambda e: e.dma_start(
            out=taps["attn"], in_=attn_sb[:, :, :, :].rearrange("p a b c -> p (a b c)")),
            deps=[aev[(hb2, NB - 1)] for hb2 in range(4)], inc=16, tag="tap_at")

    if PHASES <= 6:
        return _emit_all(nc, pl, sems, [aev[(hb2, NB - 1)] for hb2 in range(4)])

    st_a2 = A("sync", lambda e: e.dma_start(
        out=a2_in.ap(), in_=attn_sb[:, :, :, :].rearrange("p a b c -> p (a b c)")),
        deps=[aev[(hb2, NB - 1)] for hb2 in range(4)], inc=16, tag="st_a2")
    ag2 = A("gpsimd", lambda e: e.collective_compute(
        "AllGather", ALU.bypass, replica_groups=[list(range(R))],
        ins=[a2_in.ap().opt()], outs=[a2_out.ap().opt()]),
        deps=[st_a2], is_async=True, tag="ag2")
    ag2 = A("gpsimd", lambda e: e.collective_compute(
        "AllReduce", ALU.add, replica_groups=[list(range(R))],
        ins=[bar_in.ap().opt()], outs=[bar_out.ap().opt()]),
        deps=[ag2], is_async=True, tag="ag2bar")

    wo_ld = {}

    sreg2 = {}

    def mkreg2(e):
        r = e.alloc_register("oreg")
        e.reg_load(r, offs_sb[0:1, 24:25])
        sreg2["o"] = RuntimeValue(r, min_val=0, max_val=3584)
        return None
    st_reg2 = A("scalar", mkreg2, deps=[ld_c], inc=0, tag="oreg")

    a24 = a2_out.ap().rearrange("(rk p) (g c) -> rk p g c", p=P, g=4)

    def atldfn(e):
        for b in range(2):
            for rk in range(8):
                e.dma_start(
                    out=at_sb[:, b, rk * 2:rk * 2 + 2, :],
                    in_=a24[rk, :, b:b + 3:2, :][:, :, ds(sreg2["o"], 512)]
                    ).then_inc(sems["scalar"], 16)
    at_ld = A("scalar", atldfn, deps=[ag2, st_reg2, moe_hmm[-1]], inc=16 * 16, tag="atld",
              self_inc=True)

    if "ag7" in DEBUG_TAPS:
        def ag7fn(e):
            e.dma_start(out=taps["ag7"][:, 0:2048],
                        in_=a2_out.ap()[0:128, 0:2048]).then_inc(sems["sync"], 16)
            e.dma_start(out=taps["ag7"][:, 2048:4096],
                        in_=a2_in.ap()[0:128, 0:2048]).then_inc(sems["sync"], 16)
        A("sync", ag7fn, deps=[ag2], inc=32, tag="tap_ag7", self_inc=True)

    if "at7" in DEBUG_TAPS:
        A("sync", lambda e: e.dma_start(
            out=taps["at7"], in_=at_sb[:, :, :, :].rearrange("p a b c -> p (a b c)")),
            deps=[at_ld], inc=16, tag="tap_at7")

    op_mm, op_fin, o_st = {}, {}, {}
    xf_ld, g_st = {}, {}
    for ob in range(4):
        dwo = [asg[(3, NB - 1, NB - 1)]]
        if ob >= 2:
            dwo.append(op_mm[(ob - 2) * 8 + 7])
        wo_ld[ob] = A("gpsimd", (lambda ob: lambda e: e.dma_start(
            out=wob[:, ob % 2, :, :],
            in_=wo_d[ob].rearrange("p (hc f) -> p hc f", hc=DC)))(ob),
            deps=dwo, inc=16, tag=f"wo{ob}")
        for b in range(2):
            for ts in range(4):
                u = ob * 8 + b * 4 + ts
                slot = u % 4
                dop = [at_ld, wo_ld[ob], consts, ld_bo]
                if u >= 4:
                    dop.append(op_fin[u - 4])
                if u < 4:
                    dop.append(aexp[(3, NB - 1, NB - 1)][1])

                def opfn(b=b, ts=ts, ob=ob, slot=slot):
                    def f(e):
                        for h in range(NH):
                            MM(e, psA[:, slot, :], at_sb[:, b, h, ts * 128:ts * 128 + 128],
                               wob[:, ob % 2, h, :], h == 0, False)
                        return MM(e, psA[:, slot, :], onescol[:, :],
                                  bout_b[0:1, ob * 512:ob * 512 + 512], False, True)
                    return f
                op_mm[u] = A("tensor", opfn(), deps=dop, tag=f"op{u}")

                xd = x0_d if b == 0 else x1_d
                dx = [st_a2] if u < 2 else [o_st[u - 2]]
                if u < 2:
                    dx.append(aev[(3, NB - 1)])
                xf_ld[u] = A("sync", (lambda xd, ts, ob, u: lambda e: e.dma_start(
                    out=xrb[:, u % 2, :],
                    in_=xd[ts * 128:ts * 128 + 128, ob * 512:ob * 512 + 512]))(xd, ts, ob, u),
                    deps=dx, inc=16, tag=f"xf{u}")

                dg = [moe_yac[-1]]
                if u >= 2:
                    dg.append(op_fin[u - 2])
                else:
                    dg.append(asg[(3, NB - 1, NB - 1)])

                def gfn(b=b, ts=ts, ob=ob, u=u):
                    def f(e):
                        tc = b * 4 + ts
                        return e.activation(gsc[:, u % 2, :],
                                            y_acc[:, tc, ob * 512:ob * 512 + 512], AF.Silu)
                    return f
                g_st[u] = A("scalar", gfn(), deps=dg, tag=f"g{u}")

                def finfn(u=u, slot=slot):
                    def f(e):
                        e.tensor_tensor(sgs[:, u % 2, :], psA[:, slot, :], gsc[:, u % 2, :],
                                        ALU.mult)
                        e.drain()
                        ins = e.tensor_tensor(osb[:, u % 2, :], sgs[:, u % 2, :],
                                              xrb[:, u % 2, :], ALU.add)
                        e.drain()
                        return ins
                    return f
                op_fin[u] = A("vector", finfn(), deps=[op_mm[u], g_st[u], xf_ld[u]],
                              tag=f"fi{u}")

                o_st[u] = A("sync", (lambda b, ts, ob, u: lambda e: e.dma_start(
                    out=out_d[b, ts * 128:ts * 128 + 128, ob * 512:ob * 512 + 512],
                    in_=osb[:, u % 2, :]))(b, ts, ob, u),
                    deps=[op_fin[u]], inc=16, tag=f"os{u}")

    return _emit_all(nc, pl, sems, [o_st[31]])


def kernel(**inputs):
    in_maps = host_layouts(inputs)
    key = (PHASES, tuple(sorted(DEBUG_TAPS)))
    if key not in _CACHE:
        _CACHE[key] = build_program()
    nc = _CACHE[key]
    res = run_bass_kernel_spmd(nc, in_maps, core_ids=list(range(R)))
    out = np.zeros((2, 4096, D), np.float32)
    for r in range(R):
        o = np.asarray(res.results[r]["out"], np.float32)
        out[0, TB * r:TB * (r + 1), :] = o[0]
        out[1, TB * r:TB * (r + 1), :] = o[1]
    if DEBUG_TAPS:
        kernel.taps = [{k: np.asarray(v) for k, v in res.results[r].items()
                        if k.startswith("t_")} for r in range(R)]
    return out


# version 12 (final)
# speedup vs baseline: 1.0872x; 1.0872x over previous
import numpy as np
import ml_dtypes

import concourse.bass as bass
import concourse.mybir as mybir
from concourse.bass import ds
from concourse.bass_utils import run_bass_kernel_spmd
from concourse.expressions import RuntimeValue

F32 = mybir.dt.float32
BF16 = mybir.dt.bfloat16
I32 = mybir.dt.int32
AF = mybir.ActivationFunctionType
ALU = mybir.AluOpType
AX = mybir.AxisListType

R = 8
P = 128
TB = 512
T = 1024
D = 2048
DC = 16
NH = 16
HD = 128
ROT = 32
FF = 1408
FC = 11
NE = 8
NB = 8
EPS = 1e-5
NEGBIG = -1.0e9

ENGINES = ("sync", "gpsimd", "tensor", "vector", "scalar")

DEBUG_TAPS = ()
PHASES = 7
MOE_SERIAL = False
AG2_DELAY = False
_CACHE = {}


class Step:
    __slots__ = ("eng", "fn", "deps", "inc", "val", "tag", "is_async", "self_inc")

    def __init__(self, eng, fn, deps, inc, tag, is_async, self_inc):
        self.eng = eng
        self.fn = fn
        self.deps = [d for d in deps if d is not None]
        self.inc = inc
        self.tag = tag
        self.is_async = is_async
        self.self_inc = self_inc


class Plan:
    def __init__(self):
        self.by_eng = {e: [] for e in ENGINES}

    def add(self, eng, fn, deps=(), inc=1, tag="", is_async=None, self_inc=False):
        if is_async is None:
            is_async = inc >= 16
        st = Step(eng, fn, deps, inc, tag, is_async, self_inc)
        lst = self.by_eng[eng]
        prev = lst[-1].val if lst else 0
        st.val = prev + inc
        lst.append(st)
        return st

    def emit(self, eng_name, eng, sems):
        hiwater = {e: 0 for e in ENGINES}
        for st in self.by_eng[eng_name]:
            need = {}
            for d in st.deps:
                if d.eng == eng_name and not d.is_async:
                    continue
                need[d.eng] = max(need.get(d.eng, 0), d.val)
            for pe, v in sorted(need.items()):
                if v > hiwater[pe]:
                    eng.wait_ge(sems[pe], v)
                    hiwater[pe] = v
            last = st.fn(eng)
            if st.inc and not st.self_inc:
                assert last is not None, f"step {st.tag} returned no instruction"
                last.then_inc(sems[eng_name], st.inc)


def host_layouts(inputs):
    x = np.asarray(inputs["L0_attn_hidden_states"], np.float32)
    w_qkv = np.asarray(inputs["L0_attn_w_qkv"], np.float32)
    b_qkv = np.asarray(inputs["L0_attn_b_qkv"], np.float32)
    w_out = np.asarray(inputs["L0_attn_w_out"], np.float32)
    b_out = np.asarray(inputs["L0_attn_b_out"], np.float32)
    Wg = np.asarray(inputs["L0_ffn_Wg"], np.float32)
    W1 = np.asarray(inputs["L0_ffn_W1"], np.float32)
    W2 = np.asarray(inputs["L0_ffn_W2"], np.float32)
    W3 = np.asarray(inputs["L0_ffn_W3"], np.float32)
    c = np.ascontiguousarray

    wq = w_qkv.reshape(D, NH, 3, HD)
    wqk = c(wq[:, :, 0:2, :].reshape(D, NH * 2 * HD))
    wv = c(wq[:, :, 2, :].reshape(D, NH * HD))
    bq = b_qkv.reshape(NH, 3, HD)
    bqk = c(bq[:, 0:2, :].reshape(NH * 2, HD).T)
    bv = c(bq[:, 2, :].reshape(NH, HD).T)

    wqks = c(wqk.reshape(DC, P, 16, 256).transpose(2, 1, 0, 3).reshape(16, P, DC * 256))
    wvs = c(wv.reshape(DC, P, 8, 256).transpose(2, 1, 0, 3).reshape(8, P, DC * 256))
    wgs = c(Wg.T.reshape(DC, P, NE).transpose(1, 0, 2).reshape(P, DC * NE))
    w1s = c(W1.transpose(0, 2, 1).reshape(NE, DC, P, FC, 128).transpose(0, 3, 2, 1, 4)
            .reshape(NE, FC, P, DC * 128))
    w2s = c(W2.transpose(0, 2, 1).reshape(NE, DC, P, FC, 128).transpose(0, 3, 2, 1, 4)
            .reshape(NE, FC, P, DC * 128))
    w3s = c(W3.transpose(0, 2, 1).reshape(NE, FC, P, 4, 512).transpose(0, 3, 2, 1, 4)
            .reshape(NE, 4, P, FC * 512))
    wos = c(w_out.reshape(DC, P, 4, 512).transpose(2, 1, 0, 3).reshape(4, P, DC * 512))

    inv = 1.0 / (10000.0 ** (np.arange(0, ROT, 2, dtype=np.float64) / ROT))
    pos = np.arange(4096, dtype=np.float64)
    ang = pos[:, None] * inv[None, :]
    emb = np.concatenate([ang, ang], axis=1)
    cos_full = np.cos(emb).astype(np.float32).T
    sin2_full = np.sin(emb).astype(np.float32).T
    sin2_full[0:16, :] *= -1.0

    jj = np.arange(4)[:, None, None]
    pp = np.arange(P)[None, :, None]
    qq = np.arange(512)[None, None, :]
    masks = np.where(jj * P + pp <= qq, np.float32(0.0), np.float32(NEGBIG))
    masks = c(masks.astype(ml_dtypes.bfloat16).transpose(1, 0, 2)
              .reshape(P, 4 * 512).view(np.uint16))

    ident = np.zeros((P, 64), np.float32)
    ident[:, :] = np.eye(P, dtype=ml_dtypes.bfloat16).view(np.float32)

    in_maps = []
    for r in range(R):
        sl = slice(TB * r, TB * (r + 1))
        offs = np.zeros((1, 32), np.int64)
        for hl in range(2):
            for b in range(2):
                hb = hl * 2 + b
                h = 2 * r + hl
                offs[0, hb] = (h * 2 + b) * P * 512
                offs[0, 4 + hb] = (4096 + (h * 2 + b) * P) * 512
                for ts in range(4):
                    offs[0, 8 + hb * 4 + ts] = ((h * 2 + b) * TB + ts * P) * HD
        offs[0, 24] = r * TB
        in_maps.append({
            "x0": c(x[0, sl, :]), "x1": c(x[1, sl, :]),
            "wqk": wqks, "wv": wvs, "bqk": bqk, "bv": bv,
            "bout": b_out.reshape(1, D),
            "wg": wgs, "w1s": w1s, "w2s": w2s, "w3s": w3s, "wos": wos,
            "cos_t": c(cos_full[:, sl]), "sin2_t": c(sin2_full[:, sl]),
            "masks_u16": masks,
            "ident": ident,
            "offs": offs.astype(np.int32),
        })
    return in_maps


def _emit_all(nc, pl, sems, tail_steps):
    pl.add("sync", lambda e: None, deps=list(tail_steps), inc=0, tag="final")

    with nc.Block() as block:
        @block.sync
        def _(e):
            pl.emit("sync", e, sems)

        @block.gpsimd
        def _(e):
            pl.emit("gpsimd", e, sems)

        @block.tensor
        def _(e):
            pl.emit("tensor", e, sems)

        @block.vector
        def _(e):
            pl.emit("vector", e, sems)

        @block.scalar
        def _(e):
            pl.emit("scalar", e, sems)

    nc.finalize()
    return nc


def build_program():
    nc = bass.Bass("TRN2", target_bir_lowering=False)
    pl = Plan()
    A = pl.add

    def din(name, shape, dt=F32):
        return nc.dram_tensor(name, list(shape), dt, kind="ExternalInput").ap()

    x0_d = din("x0", (TB, D))
    x1_d = din("x1", (TB, D))
    wqk_d = din("wqk", (16, P, DC * 256))
    wv_d = din("wv", (8, P, DC * 256))
    bqk_d = din("bqk", (P, 32))
    bv_d = din("bv", (P, 16))
    bout_d = din("bout", (1, D))
    wg_d = din("wg", (P, DC * NE))
    w1_d = din("w1s", (NE, FC, P, DC * 128))
    w2_d = din("w2s", (NE, FC, P, DC * 128))
    w3_d = din("w3s", (NE, 4, P, FC * 512))
    wo_d = din("wos", (4, P, DC * 512))
    cos_d = din("cos_t", (ROT, TB))
    sin_d = din("sin2_t", (ROT, TB))
    mask_d = din("masks_u16", (P, 4 * 512), mybir.dt.uint16)
    id_d = din("ident", (P, 64))
    offs_d = din("offs", (1, 32), I32)

    out_d = nc.dram_tensor("out", [2, TB, D], F32, kind="ExternalOutput").ap()
    taps = {}
    for tname, shape, dt in [
        ("xn", [P, 8 * D], BF16), ("q", [P, NH * T], BF16), ("k", [P, NH * T], BF16),
        ("v", [P, 32 * 4 * HD], BF16), ("w", [P, 8 * NE], F32), ("y", [P, 8 * D], F32),
        ("attn", [P, 4 * NB * 512], BF16), ("xnt", [P, DC * T], BF16),
        ("kv6", [P, 3 * NB * 512], BF16), ("at7", [P, 2 * NH * 512], BF16),
        ("ag7", [P, 4096], BF16),
    ]:
        if tname in DEBUG_TAPS:
            taps[tname] = nc.dram_tensor("t_" + tname, shape, dt, kind="ExternalOutput").ap()

    qk_in = nc.dram_tensor("qk_in", [8192, 512], BF16, kind="Internal")
    qk_out = nc.dram_tensor("qk_out", [65536, 512], BF16, kind="Internal", addr_space="Shared")
    v_in = nc.dram_tensor("v_in", [16384, 128], BF16, kind="Internal")
    v_out = nc.dram_tensor("v_out", [131072, 128], BF16, kind="Internal", addr_space="Shared")
    a2_in = nc.dram_tensor("a2_in", [P, 16384], BF16, kind="Internal")
    a2_out = nc.dram_tensor("a2_out", [1024, 16384], BF16, kind="Internal", addr_space="Shared")
    bar_in = nc.dram_tensor("bar_in", [1, 128], F32, kind="Internal")
    bar_out = nc.dram_tensor("bar_out", [1, 128], F32, kind="Internal")

    KB = 1024
    ar_t = nc.alloc_sbuf_tensor("ar_t", [P, 32 * KB], mybir.dt.uint8)
    ar_y = nc.alloc_sbuf_tensor("ar_y", [P, 64 * KB], mybir.dt.uint8)
    ar_m = nc.alloc_sbuf_tensor("ar_m", [P, 64 * KB], mybir.dt.uint8)
    ar_x = nc.alloc_sbuf_tensor("ar_x", [P, 32 * KB], mybir.dt.uint8)

    def view(arena, off_kb, nbytes_kb, dt, shape):
        apv = arena[:, off_kb * KB:(off_kb + nbytes_kb) * KB].bitcast(dt)
        if len(shape) == 2:
            assert apv.shape[1] == shape[1], (apv.shape, shape)
            return apv
        pat = "p (" + " ".join(f"d{i}" for i in range(1, len(shape))) + ") -> p " + \
              " ".join(f"d{i}" for i in range(1, len(shape)))
        kw = {f"d{i}": shape[i] for i in range(1, len(shape))}
        return apv.rearrange(pat, **kw)

    xnT = view(ar_t, 0, 32, BF16, (P, DC, T))
    at_sb = view(ar_t, 0, 32, BF16, (P, 2, NH, 512))

    xn_tm = view(ar_y, 0, 32, BF16, (P, 8, D))
    Qfm = view(ar_y, 0, 32, BF16, (P, NH, T))
    Kfm = view(ar_y, 32, 32, BF16, (P, NH, T))
    y_acc = view(ar_y, 0, 64, F32, (P, 8, D))

    xst = view(ar_m, 0, 16, F32, (P, 2, D))
    Vsb = view(ar_m, 0, 32, BF16, (P, 32, 4, HD))
    wqkb = view(ar_m, 32, 16, BF16, (P, 2, DC, 256))
    qkscr = view(ar_m, 48, 4, F32, (P, 2, 512))
    vtmp = view(ar_m, 52, 2, BF16, (P, 2, 512))
    rsh = view(ar_m, 54, 2, F32, (ROT, 512))[0:ROT]
    rt1 = view(ar_m, 56, 2, F32, (ROT, 512))[0:ROT]
    rt2 = view(ar_m, 58, 2, F32, (ROT, 512))[0:ROT]
    hwT = view(ar_m, 0, 22, BF16, (P, FC, T))
    w1b = view(ar_m, 22, 8, BF16, (P, 2, DC, 128))
    w2b = view(ar_m, 30, 8, BF16, (P, 2, DC, 128))
    w3b = view(ar_m, 38, 22, BF16, (P, 2, FC, 512))
    sil = view(ar_m, 60, 2, BF16, (P, 2, 512))
    tts = view(ar_m, 62, 2, BF16, (P, 2, 512))
    e4 = view(ar_m, 0, 32, BF16, (P, 16384))
    kbig = view(ar_m, 32, 8, BF16, (P, NB, 512))
    vbig = view(ar_m, 40, 8, BF16, (P, NB, 4, HD))
    qbig = view(ar_m, 48, 8, BF16, (P, NB, 512))
    al_sb = view(ar_m, 56, 1, BF16, (P, 512))
    rb_sb = view(ar_m, 57, 2, F32, (P, 512))
    wob = view(ar_m, 0, 32, BF16, (P, 2, DC, 512))
    gsc = view(ar_m, 32, 4, F32, (P, 2, 512))
    sgs = view(ar_m, 36, 4, F32, (P, 2, 512))
    xrb = view(ar_m, 40, 4, F32, (P, 2, 512))
    osb = view(ar_m, 44, 4, F32, (P, 2, 512))

    attn_sb = view(ar_x, 0, 32, BF16, (P, 4, NB, 512))

    sbt = nc.alloc_sbuf_tensor
    bqk_sb = sbt("bqk_sb", [P, 32], F32)
    bv_sb = sbt("bv_sb", [P, 16], F32)
    bout_b = sbt("bout_b", [1, D], BF16)
    cos_sb = sbt("cos_sb", [ROT, TB], F32)
    sin_sb = sbt("sin_sb", [ROT, TB], F32)
    mask_sb = sbt("mask_sb", [P, 4, 512], BF16)
    identb = sbt("identb", [P, P], BF16)
    onescol = sbt("onescol", [1, P], BF16)
    onesP = sbt("onesP", [P, P], BF16)
    wg_sb = sbt("wg_sb", [P, DC, NE], BF16)
    offs_sb = sbt("offs_sb", [1, 32], I32)
    bnbuf = sbt("bnbuf", [P, 8, 24], F32)
    mv = sbt("mv", [P, 8, 2], F32)
    rstd = sbt("rstd", [P, 8], F32)
    negmrs = sbt("negmrs", [P, 8], F32)
    vareps = sbt("vareps", [P, 8], F32)
    stdv = sbt("stdv", [P, 8], F32)
    rsc = sbt("rsc", [P, 8, NE], F32)
    m1s = sbt("m1s", [P, 8], F32)
    m2s = sbt("m2s", [P, 8], F32)
    d12 = sbt("d12", [P, 8], F32)
    p1s = sbt("p1s", [P, 8], F32)
    p2s = sbt("p2s", [P, 8], F32)
    eqa = sbt("eqa", [P, NE], F32)
    eqb = sbt("eqb", [P, NE], F32)
    eqm = sbt("eqm", [P, NE], F32)
    wful = sbt("wful", [P, 8, NE], F32)

    psA = nc.alloc_psum_tensor("psA", [P, 4, 512], F32)
    ps4a = nc.alloc_psum_tensor("ps4a", [P, 512], F32)
    ps4b = nc.alloc_psum_tensor("ps4b", [P, 512], F32)
    psS = nc.alloc_psum_tensor("psS", [P, 512], F32)
    psTs = [ps4a[:, :].bitcast(BF16)[:, 0:128], ps4b[:, :].bitcast(BF16)[:, 0:128]]

    sems = {e: nc.alloc_semaphore("sem_" + e) for e in ENGINES}

    def MM(e, out, lhsT, rhs, start, stop):
        return e.matmul(out, lhsT, rhs, start=start, stop=stop, skip_group_check=True)

    def ldcfn(e):
        sm = sems["sync"]
        e.dma_start(out=bqk_sb[:, :], in_=bqk_d).then_inc(sm, 16)
        e.dma_start(out=bv_sb[:, :], in_=bv_d).then_inc(sm, 16)
        e.dma_start(out=cos_sb[:, :], in_=cos_d).then_inc(sm, 16)
        e.dma_start(out=sin_sb[:, :], in_=sin_d).then_inc(sm, 16)
        e.dma_start(out=mask_sb[:, :, :],
                    in_=mask_d.bitcast(BF16).rearrange("p (m f) -> p m f", m=4)).then_inc(sm, 16)
        e.dma_start(out=identb[:, :], in_=id_d.bitcast(BF16)).then_inc(sm, 16)
        e.dma_start(out=offs_sb[:, :], in_=offs_d).then_inc(sm, 16)
    ld_c = A("sync", ldcfn, inc=16 * 7, tag="ld_c", self_inc=True)
    ld_wg = A("gpsimd", lambda e: e.dma_start(
        out=wg_sb[:, :, :], in_=wg_d.rearrange("p (dc e2) -> p dc e2", dc=DC)),
        inc=16, tag="ld_wg")
    ld_bo = A("gpsimd", lambda e: e.dma_start(out=bout_b[:, :], in_=bout_d),
              inc=16, tag="ld_bo")
    consts = A("vector", lambda e: (
        e.memset(onescol[:, :], 1.0),
        e.memset(onesP[:, :], 1.0))[-1],
        deps=[], tag="consts")

    x_ld = []
    for tc in range(8):
        xd = x0_d if tc < 4 else x1_d
        ts = tc % 4
        deps = []
        if tc >= 2:
            pass
        x_ld.append(A("sync", (lambda xd, ts, tc: lambda e: e.dma_start(
            out=xst[:, tc % 2, :], in_=xd[ts * 128:(ts + 1) * 128, :]))(xd, ts, tc),
            deps=[], inc=16, tag=f"xld{tc}"))

    ln_st = []
    s_xn = []
    for tc in range(8):
        def stfn(tc=tc):
            def f(e):
                for i in range(4):
                    e.bn_stats(bnbuf[:, tc, 6 * i:6 * i + 6],
                               xst[:, tc % 2, 512 * i:512 * i + 512])
                e.drain()
                e.bn_aggr(mv[:, tc, :], bnbuf[:, tc, :])
                e.drain()
                ins = e.tensor_scalar_add(vareps[:, tc:tc + 1], mv[:, tc, 1:2], float(EPS))
                e.drain()
                return ins
            return f
        ln_st.append(A("vector", stfn(), deps=[x_ld[tc]], tag=f"lnst{tc}"))
        s_sq = A("scalar", (lambda tc: lambda e: e.activation(
            stdv[:, tc:tc + 1], vareps[:, tc:tc + 1], AF.Sqrt))(tc),
            deps=[ln_st[tc]], tag=f"lnsq{tc}")

        def xnfn(tc=tc):
            def f(e):
                e.reciprocal(rstd[:, tc:tc + 1], stdv[:, tc:tc + 1])
                e.drain()
                e.scalar_tensor_tensor(negmrs[:, tc:tc + 1], mv[:, tc, 0:1], -1.0,
                                       rstd[:, tc:tc + 1], ALU.mult, ALU.mult)
                e.drain()
                ins = e.tensor_scalar(xn_tm[:, tc, :], xst[:, tc % 2, :],
                                      rstd[:, tc:tc + 1], negmrs[:, tc:tc + 1],
                                      ALU.mult, ALU.add)
                e.drain()
                return ins
            return f
        s_xn.append(A("vector", xnfn(), deps=[s_sq, x_ld[tc]], tag=f"xn{tc}"))

    for tc in range(2, 8):
        x_ld[tc].deps.append(ln_st[tc - 2])
        x_ld[tc].deps.append(s_xn[tc - 2])

    if "xn" in DEBUG_TAPS:
        A("sync", lambda e: e.dma_start(out=taps["xn"],
                                        in_=xn_tm[:, :, :].rearrange("p a b -> p (a b)")),
          deps=s_xn, inc=16, tag="tap_xn")

    if PHASES <= 1:
        return _emit_all(nc, pl, sems, s_xn)

    tr_mm, tr_ev = [], []
    for i in range(128):
        tc, dc = i // DC, i % DC
        slot = i % 2
        deps = [s_xn[tc]]
        if i >= 2:
            deps.append(tr_ev[i - 2])
        tr_mm.append(A("tensor", (lambda tc, dc, slot: lambda e: e.transpose(
            psTs[slot], xn_tm[:, tc, dc * 128:dc * 128 + 128], identb[:, :]))(tc, dc, slot),
            deps=deps, tag=f"trm{i}"))
        tr_ev.append(A("vector", (lambda tc, dc, slot: lambda e: e.tensor_copy(
            xnT[:, dc, tc * 128:tc * 128 + 128], psTs[slot]))(tc, dc, slot),
            deps=[tr_mm[i]], tag=f"tre{i}"))

    if "xnt" in DEBUG_TAPS:
        tp = A("sync", lambda e: e.dma_start(
            out=taps["xnt"], in_=xnT[:, :, :].rearrange("p a b -> p (a b)")),
            deps=[tr_ev[-1]], inc=16, tag="tap_xnt")
    if PHASES <= 2:
        return _emit_all(nc, pl, sems, [tr_ev[-1]])

    wq_ld, qkv_mm = [], []
    qk_evac, rope_done, v_evac = {}, {}, {}
    vtr_mm, vtr_ev = [], []
    tile_free = {}

    for gi in range(24):
        wsrc = wqk_d[gi] if gi < 16 else wv_d[gi - 16]
        wdeps = [s_xn[-1]] if gi < 2 else [qkv_mm[(gi - 2) * 4 + 3]]
        wq_ld.append(A("gpsimd", (lambda wsrc, gi: lambda e: e.dma_start(
            out=wqkb[:, gi % 2, :, :],
            in_=wsrc.rearrange("p (dc f) -> p dc f", dc=DC)))(wsrc, gi),
            deps=wdeps, inc=16, tag=f"wq{gi}"))
        for ci in range(2):
            for bp in range(2):
                idx = gi * 4 + ci * 2 + bp
                slot = ci * 2 + bp
                deps = [wq_ld[gi]]
                if idx == 0:
                    deps.append(tr_ev[-1])
                if idx - 4 >= 0:
                    deps.append(tile_free[idx - 4])

                def mmfn(gi=gi, ci=ci, bp=bp, slot=slot):
                    def f(e):
                        ins = None
                        for dc in range(DC):
                            ins = MM(e, psA[:, slot, :],
                                     wqkb[:, gi % 2, dc, ci * 128:ci * 128 + 128],
                                     xnT[:, dc, bp * 512:bp * 512 + 512],
                                     dc == 0, dc == DC - 1)
                        return ins
                    return f
                st_mm = A("tensor", mmfn(), deps=deps, tag=f"qm{idx}")
                qkv_mm.append(st_mm)

                if gi < 16:
                    cch = gi * 2 + ci
                    h, isk = cch // 2, cch % 2
                    dst = Kfm if isk else Qfm
                    evdeps = [st_mm, ld_c, tr_ev[-1]]
                    if idx - 2 >= 0 and (idx - 2) in rope_done:
                        evdeps.append(rope_done[idx - 2])
                        evdeps.append(qk_evac[idx - 2][1])

                    def evacfn(cch=cch, bp=bp, slot=slot):
                        def f(e):
                            ins = e.tensor_scalar(qkscr[:, bp, :], psA[:, slot, :],
                                                  bqk_sb[:, cch:cch + 1], None, ALU.add)
                            e.drain()
                            return ins
                        return f
                    st_ev = A("vector", evacfn(), deps=evdeps, tag=f"qe{idx}")

                    def pcastfn(bp=bp, dst=dst, h=h):
                        def f(e):
                            e.tensor_copy(dst[32:64, h, bp * 512:bp * 512 + 512],
                                          qkscr[32:64, bp, :])
                            ins = e.tensor_copy(dst[64:128, h, bp * 512:bp * 512 + 512],
                                                qkscr[64:128, bp, :])
                            e.drain()
                            return ins
                        return f
                    st_pc = A("vector", pcastfn(), deps=[st_ev], tag=f"qp{idx}")
                    qk_evac[idx] = (st_ev, st_pc)

                    def ropefn(bp=bp, h=h, dst=dst):
                        def f(e):
                            e.stream_shuffle(rsh[:, :], qkscr[0:ROT, bp, :],
                                             [(i + 16) % 32 for i in range(ROT)])
                            e.tensor_tensor(rt1[:, :], qkscr[0:ROT, bp, :], cos_sb[:, :], ALU.mult)
                            e.tensor_tensor(rt2[:, :], rsh[:, :], sin_sb[:, :], ALU.mult)
                            e.drain()
                            ins = e.tensor_tensor(dst[0:ROT, h, bp * 512:bp * 512 + 512],
                                                  rt1[:, :], rt2[:, :], ALU.add)
                            e.drain()
                            return ins
                        return f
                    st_rp = A("vector", ropefn(), deps=[st_ev], tag=f"qr{idx}")
                    rope_done[idx] = st_rp
                    tile_free[idx] = st_rp
                else:
                    hv = (gi - 16) * 2 + ci
                    vdeps = [st_mm, ld_c]
                    if idx - 2 >= 0 and (idx - 2) in v_evac:
                        vdeps.append(tile_free[idx - 2])

                    def vefn(hv=hv, bp=bp, slot=slot):
                        def f(e):
                            ins = e.tensor_scalar(vtmp[:, bp, :], psA[:, slot, :],
                                                  bv_sb[:, hv:hv + 1], None, ALU.add)
                            e.drain()
                            return ins
                        return f
                    st_ve = A("vector", vefn(), deps=vdeps, tag=f"ve{idx}")
                    v_evac[idx] = st_ve
                    last = None
                    for ts in range(4):
                        k = len(vtr_mm)
                        tslot = k % 2
                        dv = [st_ve]
                        if k >= 2:
                            dv.append(vtr_ev[k - 2])
                        else:
                            dv.append(tr_ev[-1])
                        vtr_mm.append(A("tensor", (lambda bp, ts, tslot: lambda e: e.transpose(
                            psTs[tslot],
                            vtmp[:, bp, ts * 128:ts * 128 + 128], identb[:, :]))(bp, ts, tslot),
                            deps=dv, tag=f"vt{k}"))
                        vtr_ev.append(A("vector", (lambda hv, bp, ts, tslot: lambda e: e.tensor_copy(
                            Vsb[:, hv * 2 + bp, ts, :], psTs[tslot]))(hv, bp, ts, tslot),
                            deps=[vtr_mm[-1]], tag=f"vv{k}"))
                        last = vtr_ev[-1]
                    tile_free[idx] = last

    rt_mm, rt_ev = [], []
    for tc in range(8):
        deps = [tr_ev[-1], ld_wg]
        if tc >= 1:
            deps.append(rt_ev[tc - 1])

        def rfn(tc=tc):
            def f(e):
                ins = None
                for dc in range(DC):
                    ins = MM(e, psS[:, 0:8], xnT[:, dc, tc * 128:tc * 128 + 128],
                             wg_sb[:, dc, :], dc == 0, dc == DC - 1)
                return ins
            return f
        rt_mm.append(A("tensor", rfn(), deps=deps, tag=f"rm{tc}"))
        rt_ev.append(A("vector", (lambda tc: lambda e: e.tensor_copy(
            rsc[:, tc, :], psS[:, 0:8]))(tc), deps=[rt_mm[tc]], tag=f"re{tc}"))

    def rt_a(e):
        for tc in range(8):
            e.tensor_reduce(m1s[:, tc:tc + 1], rsc[:, tc, :], AX.X, ALU.max)
        e.drain()
        for tc in range(8):
            e.tensor_scalar(eqa[:, :], rsc[:, tc, :], m1s[:, tc:tc + 1], None, ALU.is_equal)
            e.drain()
            e.scalar_tensor_tensor(eqm[:, :], eqa[:, :], NEGBIG, rsc[:, tc, :], ALU.mult, ALU.add)
            e.drain()
            e.tensor_reduce(m2s[:, tc:tc + 1], eqm[:, :], AX.X, ALU.max)
            e.drain()
        ins = e.tensor_sub(d12[:, :], m1s[:, :], m2s[:, :])
        e.drain()
        return ins
    s_t2a = A("vector", rt_a, deps=[rt_ev[-1]], tag="rta")
    s_sig = A("scalar", lambda e: e.activation(p1s[:, :], d12[:, :], AF.Sigmoid),
              deps=[s_t2a], tag="rsg")

    def rt_b(e):
        e.tensor_scalar(p2s[:, :], p1s[:, :], -1.0, 1.0, ALU.mult, ALU.add)
        e.drain()
        for tc in range(8):
            e.tensor_scalar(eqa[:, :], rsc[:, tc, :], m1s[:, tc:tc + 1], None, ALU.is_equal)
            e.tensor_scalar(eqb[:, :], rsc[:, tc, :], m2s[:, tc:tc + 1], None, ALU.is_equal)
            e.drain()
            e.tensor_scalar(eqm[:, :], eqa[:, :], p1s[:, tc:tc + 1], None, ALU.mult)
            e.tensor_scalar(eqb[:, :], eqb[:, :], p2s[:, tc:tc + 1], None, ALU.mult)
            e.drain()
            ins = e.tensor_add(wful[:, tc, :], eqm[:, :], eqb[:, :])
            e.drain()
        return ins
    s_t2b = A("vector", rt_b, deps=[s_sig], tag="rtb")
    if "w" in DEBUG_TAPS:
        A("sync", lambda e: e.dma_start(out=taps["w"],
                                        in_=wful[:, :, :].rearrange("p a b -> p (a b)")),
          deps=[s_t2b], inc=16, tag="tap_w")

    if PHASES <= 3:
        return _emit_all(nc, pl, sems, [rope_done[max(rope_done)], qk_evac[max(qk_evac)][1],
                                        vtr_ev[-1], moe_yac_dummy_t2b := s_t2b])

    all_rope_last = rope_done[max(rope_done)]
    st_q = A("sync", lambda e: e.dma_start(
        out=qk_in.ap()[0:4096, :].rearrange("(g p) c -> p g c", p=P),
        in_=Qfm[:, :, :].rearrange("p h (b c) -> p (h b) c", b=2)),
        deps=[all_rope_last, qk_evac[max(qk_evac)][1]], inc=16, tag="st_q")
    st_k = A("sync", lambda e: e.dma_start(
        out=qk_in.ap()[4096:8192, :].rearrange("(g p) c -> p g c", p=P),
        in_=Kfm[:, :, :].rearrange("p h (b c) -> p (h b) c", b=2)),
        deps=[], inc=16, tag="st_k")
    st_v = A("sync", lambda e: e.dma_start(
        out=v_in.ap().rearrange("(g ts p) hd -> p g ts hd", p=P, ts=4),
        in_=Vsb[:, :, :, :]),
        deps=[vtr_ev[-1]], inc=16, tag="st_v")
    if "q" in DEBUG_TAPS:
        def tqkfn(e):
            e.dma_start(out=taps["q"],
                        in_=Qfm[:, :, :].rearrange("p a b -> p (a b)")).then_inc(sems["sync"], 16)
            e.dma_start(out=taps["k"],
                        in_=Kfm[:, :, :].rearrange("p a b -> p (a b)")).then_inc(sems["sync"], 16)
        A("sync", tqkfn, deps=[st_q], inc=32, tag="tap_qk", self_inc=True)
    if "v" in DEBUG_TAPS:
        A("sync", lambda e: e.dma_start(
            out=taps["v"], in_=Vsb[:, :, :, :].rearrange("p a b c -> p (a b c)")),
            deps=[st_v], inc=16, tag="tap_v")

    ag1 = A("gpsimd", lambda e: e.collective_compute(
        "AllGather", ALU.bypass, replica_groups=[list(range(R))],
        ins=[qk_in.ap().opt()], outs=[qk_out.ap().opt()]),
        deps=[st_q, st_k], is_async=True, tag="ag1")
    ag1v = A("gpsimd", lambda e: e.collective_compute(
        "AllGather", ALU.bypass, replica_groups=[list(range(R))],
        ins=[v_in.ap().opt()], outs=[v_out.ap().opt()]),
        deps=[st_v], is_async=True, tag="ag1v")

    if PHASES <= 4:
        return _emit_all(nc, pl, sems, [ag1, ag1v, s_t2b])

    moe_w12, moe_w12a, moe_hmm, moe_sil, moe_hw = [], [], [], [], []
    moe_w3, moe_ymm, moe_yac = [], [], []
    _prev = [None]
    _realA = A

    def A(eng, fn, deps=(), tag="", **kw):
        mode = MOE_SERIAL
        chain = (mode == "all") or (mode == "y" and (tag[:2] in ("my", "ya", "m3"))) or                 (mode == "h" and (tag[:2] in ("mw", "mh", "ms")))
        if chain and _prev[0] is not None:
            deps = list(deps) + [_prev[0]]
        st = _realA(eng, fn, deps=deps, tag=tag, **kw)
        if chain or mode == "all":
            _prev[0] = st
        return st
    moe_subs = []
    prev_ld = [None]
    for e_ in range(NE):
        for hc in range(FC):
            i = e_ * FC + hc
            dld = [st_v, qk_evac[max(qk_evac)][1]]
            d1 = dld + ([moe_subs[i - 2][1]] if i >= 2 else [])
            d2 = dld + ([moe_subs[i - 2][3]] if i >= 2 else [])
            if prev_ld[0] is not None:
                d1 = d1 + [prev_ld[0]]

            def w1fn(e_=e_, hc=hc, i=i):
                def f(e):
                    return e.dma_start(out=w1b[:, i % 2, :, :],
                                       in_=w1_d[e_, hc].rearrange("p (dc f) -> p dc f", dc=DC))
                return f

            def w2fn(e_=e_, hc=hc, i=i):
                def f(e):
                    return e.dma_start(out=w2b[:, i % 2, :, :],
                                       in_=w2_d[e_, hc].rearrange("p (dc f) -> p dc f", dc=DC))
                return f
            l1 = A("gpsimd", w1fn(), deps=d1, inc=16, tag=f"mw{i}a")
            l2 = A("gpsimd", w2fn(), deps=d2 + [l1], inc=16, tag=f"mw{i}b")
            moe_w12.append(l2)
            moe_w12a.append(l1)
            prev_ld[0] = l2

            dm0 = []
            if hc == 0 and e_ >= 1:
                dm0.append(moe_ymm[-1])
            subs = []
            for w_i in range(2):
                for tb2 in range(2):
                    slot = w_i * 2 + tb2
                    dms = list(dm0) + [moe_w12a[i] if w_i == 0 else moe_w12[i]]
                    if i >= 1:
                        dms.append(sil_steps_prev[tb2] if w_i == 0 else hw_steps_prev[tb2])
                    else:
                        dms.append(v_evac[max(v_evac)])

                    def hsub(i=i, w_i=w_i, tb2=tb2, slot=slot):
                        def f(e):
                            ins = None
                            wb = w1b if w_i == 0 else w2b
                            for dc in range(DC):
                                ins = MM(e, psA[:, slot, :], wb[:, i % 2, dc, :],
                                         xnT[:, dc, tb2 * 512:tb2 * 512 + 512],
                                         dc == 0, dc == DC - 1)
                            return ins
                        return f
                    subs.append(A("tensor", hsub(), deps=dms, tag=f"mh{i}_{slot}"))
            moe_hmm.append(subs[-1])
            moe_subs.append(subs)

            sil_steps = [
                A("scalar", (lambda tb2: lambda e: e.activation(
                    sil[:, tb2, :], psA[:, tb2, :], AF.Silu))(tb2),
                    deps=[subs[tb2]] + ([hw_steps_prev[tb2]] if i >= 1 else []),
                    tag=f"ms{i}_{tb2}")
                for tb2 in range(2)]
            moe_sil.append(sil_steps[-1])

            hw_steps = []
            for tb2 in range(2):
                dhw = [sil_steps[tb2], subs[2 + tb2], st_v]
                if e_ >= 1 and moe_yac:
                    dhw.append(moe_yac[(e_ - 1) * 32 + 31])

                def hwfn(hc=hc, tb2=tb2):
                    def f(e):
                        ins = e.tensor_tensor(hwT[:, hc, tb2 * 512:tb2 * 512 + 512],
                                              sil[:, tb2, :], psA[:, 2 + tb2, :], ALU.mult)
                        e.drain()
                        return ins
                    return f
                hw_steps.append(A("vector", hwfn(), deps=dhw, tag=f"hv{i}_{tb2}"))
            moe_hw.append(hw_steps[-1])
            sil_steps_prev = sil_steps
            hw_steps_prev = hw_steps

        for db in range(4):
            j = e_ * 4 + db
            dw3 = [st_v, qk_evac[max(qk_evac)][1]]
            if j >= 2:
                dw3.append(moe_ymm[(j - 2) * 8 + 7])
            if prev_ld[0] is not None:
                dw3.append(prev_ld[0])
            moe_w3.append(A("gpsimd", (lambda e_, db, j: lambda e: e.dma_start(
                out=w3b[:, j % 2, :, :],
                in_=w3_d[e_, db].rearrange("p (hc f) -> p hc f", hc=FC)))(e_, db, j),
                deps=dw3, inc=16, tag=f"m3{j}"))
            prev_ld[0] = moe_w3[-1]
            for tc in range(8):
                k = j * 8 + tc
                dy = [moe_w3[j], moe_hw[e_ * FC + FC - 1]]
                if k >= 2:
                    dy.append(moe_yac[k - 2])
                if k == 0:
                    dy.append(st_q)
                    dy.append(st_k)

                def ymmfn(j=j, tc=tc, k=k):
                    def f(e):
                        ins = None
                        pst = ps4a if k % 2 == 0 else ps4b
                        for hc in range(FC):
                            ins = MM(e, pst[:, :], hwT[:, hc, tc * 128:tc * 128 + 128],
                                     w3b[:, j % 2, hc, :], hc == 0, hc == FC - 1)
                        return ins
                    return f
                moe_ymm.append(A("tensor", ymmfn(), deps=dy, tag=f"my{k}"))

                def yacfn(e_=e_, db=db, tc=tc, k=k):
                    def f(e):
                        pst = (ps4a if k % 2 == 0 else ps4b)[:, :]
                        if e_ == 0:
                            ins = e.tensor_scalar(y_acc[:, tc, db * 512:db * 512 + 512], pst,
                                                  wful[:, tc, 0:1], None, ALU.mult)
                        else:
                            ins = e.scalar_tensor_tensor(
                                y_acc[:, tc, db * 512:db * 512 + 512], pst,
                                wful[:, tc, e_:e_ + 1],
                                y_acc[:, tc, db * 512:db * 512 + 512], ALU.mult, ALU.add)
                        e.drain()
                        return ins
                    return f
                moe_yac.append(A("vector", yacfn(), deps=[moe_ymm[k], s_t2b], tag=f"ya{k}"))

    if "y" in DEBUG_TAPS:
        A("sync", lambda e: e.dma_start(out=taps["y"],
                                        in_=y_acc[:, :, :].rearrange("p a b -> p (a b)")),
          deps=[moe_yac[-1]], inc=16, tag="tap_y")

    if PHASES <= 5:
        return _emit_all(nc, pl, sems, [moe_yac[-1], ag1, ag1v])

    A = _realA
    ag1b = A("gpsimd", lambda e: e.collective_compute(
        "AllReduce", ALU.add, replica_groups=[list(range(R))],
        ins=[bar_in.ap().opt()], outs=[bar_out.ap().opt()]),
        deps=[ag1v], is_async=True, tag="ag1bar")

    s_regs = {}

    def mkregs(e):
        regs = [e.alloc_register(f"ar{i}") for i in range(24)]
        for i in range(24):
            e.reg_load(regs[i], offs_sb[0:1, i:i + 1])
        mx = [8064 * 512] * 8 + [16256 * HD] * 16
        s_regs["v"] = [RuntimeValue(r, min_val=0, max_val=mx[i])
                       for i, r in enumerate(regs)]
        return None
    st_regs = A("sync", mkregs, deps=[ld_c, ag1b], inc=0, tag="aregs")

    kld, vld, qld = {}, {}, {}
    amm, aexp, as4, apb, asg = {}, {}, {}, {}, {}
    aal, arb, aev = {}, {}, {}
    amsk = {}
    qk3 = qk_out.ap().rearrange("(blk q) c -> blk (q c)", q=8192)
    v3 = v_out.ap().rearrange("(blk t) hd -> blk (t hd)", t=16384)

    for hb in range(4):
        dld = [st_regs]
        if hb >= 1:
            dld.append(asg[(hb - 1, NB - 1, NB - 1)])
        else:
            dld.append(moe_yac[-1])
            dld.append(moe_hmm[-1])
        qld[hb] = A("sync", (lambda hb: lambda e: e.dma_start(
            out=qbig[:, :, :],
            in_=qk3[0:NB, ds(s_regs["v"][hb], P * 512)]
            .rearrange("blk (p c) -> p blk c", p=P)))(hb),
            deps=dld, inc=16, tag=f"ql{hb}")
        kld[hb] = A("sync", (lambda hb: lambda e: e.dma_start(
            out=kbig[:, :, :],
            in_=qk3[0:NB, ds(s_regs["v"][4 + hb], P * 512)]
            .rearrange("blk (p c) -> p blk c", p=P)))(hb),
            deps=dld, inc=16, tag=f"kl{hb}")

        def vldfn(hb=hb):
            def f(e):
                for ts in range(4):
                    idx = 8 + hb * 4 + ts
                    e.dma_start(
                        out=vbig[:, :, ts, :],
                        in_=v3[0:NB, ds(s_regs["v"][idx], P * HD)]
                        .rearrange("blk (p hd) -> p blk hd", p=P)).then_inc(sems["sync"], 16)
            return f
        vld[hb] = A("sync", vldfn(), deps=dld, inc=16 * 4, tag=f"vl{hb}", self_inc=True)

        for qc in range(NB):
            key = (hb, qc)
            for kk in range(qc + 1):
                kkey = (hb, qc, kk)
                dm = [kld[hb], qld[hb]]
                if kk >= 1:
                    dm.append(aexp[(hb, qc, kk - 1)][1])
                elif qc >= 1:
                    dm.append(aexp[(hb, qc - 1, qc - 1)][1])
                elif hb >= 1:
                    dm.append(aexp[(hb - 1, NB - 1, NB - 1)][1])
                else:
                    dm.append(moe_yac[-1])
                    dm.append(moe_sil[-1])
                    dm.append(moe_hw[-1])
                if kk == 0:
                    if qc >= 1:
                        dm.append(aal[(hb, qc - 1)])
                    elif hb >= 1:
                        dm.append(aal[(hb - 1, NB - 1)])

                def ammfn(qc=qc, kk=kk, half=0):
                    def f(e):
                        ins = None
                        for jj2 in (0, 1) if half == 0 else (2, 3):
                            ins = MM(e, psA[:, jj2, :],
                                     kbig[:, kk, jj2 * 128:jj2 * 128 + 128],
                                     qbig[:, qc, :], True, True)
                        return ins
                    return f
                dmA = list(dm)
                dmB = list(dm)
                if kk >= 1:
                    dmB[2] = aexp[(hb, qc, kk - 1)][1]
                    dmA[2] = aexp[(hb, qc, kk - 1)][0]
                ammA = A("tensor", ammfn(qc, kk, 0), deps=dmA, tag=f"am{hb}{qc}{kk}a")
                ammB = A("tensor", ammfn(qc, kk, 1), deps=dmB, tag=f"am{hb}{qc}{kk}b")
                amm[kkey] = ammB

                if kk == qc:
                    def mskfn():
                        def f(e):
                            ins = e.tensor_tensor(psA[:, :, :], psA[:, :, :],
                                                  mask_sb[:, :, :], ALU.add)
                            e.drain()
                            return ins
                        return f
                    amsk[key] = A("vector", mskfn(), deps=[ammB, ld_c], tag=f"ak{hb}{qc}")
                    edepA = [amsk[key]]
                    edepB = [amsk[key]]
                else:
                    edepA = [ammA]
                    edepB = [ammB]

                expA = A("scalar", (lambda kk: lambda e: e.activation(
                    e4[:, kk * 2048:kk * 2048 + 1024],
                    psA[:, 0:2, :].rearrange("p a b -> p (a b)"), AF.Exp,
                    bias=0.0, scale=0.5))(kk),
                    deps=edepA, tag=f"ae{hb}{qc}{kk}a")
                expB = A("scalar", (lambda kk: lambda e: e.activation(
                    e4[:, kk * 2048 + 1024:kk * 2048 + 2048],
                    psA[:, 2:4, :].rearrange("p a b -> p (a b)"), AF.Exp,
                    bias=0.0, scale=0.5))(kk),
                    deps=edepB, tag=f"ae{hb}{qc}{kk}b")
                aexp[kkey] = (expA, expB)

                def s4fn(kk=kk, qc=qc):
                    def f(e):
                        ins = None
                        for jj2 in range(4):
                            ins = MM(e, ps4a[:, :], onesP[:, :],
                                     e4[:, (kk * 4 + jj2) * 512:(kk * 4 + jj2 + 1) * 512],
                                     kk == 0 and jj2 == 0, kk == qc and jj2 == 3)
                        return ins
                    return f
                ds4 = [aexp[kkey][1]]
                if kk == 0:
                    if qc >= 1:
                        ds4.append(aal[(hb, qc - 1)])
                    elif hb >= 1:
                        ds4.append(aal[(hb - 1, NB - 1)])
                    else:
                        ds4.append(moe_yac[-1])
                as4[kkey] = A("tensor", s4fn(), deps=ds4, tag=f"a4{hb}{qc}{kk}")

            def aalfn():
                def f(e):
                    with e.bass.allow_low_precision("alpha cancels in final normalize"):
                        ins = e.reciprocal(al_sb[:, :], ps4a[:, :])
                    return ins
                return f
            aal[key] = A("vector", aalfn(), deps=[as4[(hb, qc, qc)]], tag=f"aa{hb}{qc}")

            for kk in range(qc + 1):
                kkey = (hb, qc, kk)

                def pbmul(kk=kk):
                    def f(e):
                        sl3 = e4[:, kk * 2048:(kk + 1) * 2048].rearrange(
                            "p (k f) -> p k f", k=4)
                        a4 = al_sb[:, :].rearrange("p (o f) -> p o f", o=1).broadcast_to(
                            [P, 4, 512])
                        ins = e.tensor_tensor(sl3, sl3, a4, ALU.mult)
                        e.drain()
                        return ins
                    return f

                def pbsq(kk=kk):
                    def f(e):
                        sl2 = e4[:, kk * 2048:(kk + 1) * 2048]
                        ins = e.tensor_tensor(sl2, sl2, sl2, ALU.mult)
                        e.drain()
                        return ins
                    return f
                pm = A("vector", pbmul(), deps=[aal[key], aexp[kkey][1]],
                       tag=f"ab{hb}{qc}{kk}")
                if kk % 2 == 0:
                    apb[kkey] = A("scalar", (lambda kk: lambda e: e.activation(
                        e4[:, kk * 2048:(kk + 1) * 2048],
                        e4[:, kk * 2048:(kk + 1) * 2048], AF.Square))(kk),
                        deps=[pm], tag=f"as{hb}{qc}{kk}")
                else:
                    apb[kkey] = A("vector", pbsq(), deps=[pm], tag=f"ab2{hb}{qc}{kk}")

                def sgfn(kk=kk, qc=qc):
                    def f(e):
                        ins = None
                        for jj2 in range(4):
                            MM(e, ps4b[:, :], onesP[:, :],
                               e4[:, (kk * 4 + jj2) * 512:(kk * 4 + jj2 + 1) * 512],
                               kk == 0 and jj2 == 0, kk == qc and jj2 == 3)
                            ins = MM(e, psS[:, :], vbig[:, kk, jj2, :],
                                     e4[:, (kk * 4 + jj2) * 512:(kk * 4 + jj2 + 1) * 512],
                                     kk == 0 and jj2 == 0, kk == qc and jj2 == 3)
                        return ins
                    return f
                dsg = [apb[kkey], vld[hb]]
                if kk == 0:
                    if qc >= 1:
                        dsg.append(aev[(hb, qc - 1)])
                    elif hb >= 1:
                        dsg.append(aev[(hb - 1, NB - 1)])
                    else:
                        dsg.append(moe_yac[-1])
                asg[kkey] = A("tensor", sgfn(), deps=dsg, tag=f"ag{hb}{qc}{kk}")

            arb[key] = A("vector", (lambda: lambda e: e.reciprocal(rb_sb[:, :], ps4b[:, :]))(),
                         deps=[asg[(hb, qc, qc)]], tag=f"ar{hb}{qc}")

            def aevfn(hb=hb, qc=qc):
                def f(e):
                    e.drain()
                    ins = e.tensor_tensor(attn_sb[:, hb, qc, :], psS[:, :], rb_sb[:, :], ALU.mult)
                    e.drain()
                    return ins
                return f
            aev[key] = A("vector", aevfn(), deps=[arb[key]], tag=f"av{hb}{qc}")

    if "kv6" in DEBUG_TAPS:
        def kv6fn(e):
            e.dma_start(out=taps["kv6"][:, 0:NB * 512],
                        in_=kbig[:, :, :].rearrange("p a b -> p (a b)")).then_inc(sems["sync"], 16)
            e.dma_start(out=taps["kv6"][:, NB * 512:2 * NB * 512],
                        in_=qbig[:, :, :].rearrange("p a b -> p (a b)")).then_inc(sems["sync"], 16)
            e.dma_start(out=taps["kv6"][:, 2 * NB * 512:3 * NB * 512],
                        in_=vbig[:, :, :, :].rearrange("p a b c -> p (a b c)")).then_inc(sems["sync"], 16)
        A("sync", kv6fn, deps=[asg[(0, NB - 1, NB - 1)]], inc=48, tag="tap_kv6",
          self_inc=True)

    if "attn" in DEBUG_TAPS:
        A("sync", lambda e: e.dma_start(
            out=taps["attn"], in_=attn_sb[:, :, :, :].rearrange("p a b c -> p (a b c)")),
            deps=[aev[(hb2, NB - 1)] for hb2 in range(4)], inc=16, tag="tap_at")

    if PHASES <= 6:
        return _emit_all(nc, pl, sems, [aev[(hb2, NB - 1)] for hb2 in range(4)])

    st_a2 = A("sync", lambda e: e.dma_start(
        out=a2_in.ap(), in_=attn_sb[:, :, :, :].rearrange("p a b c -> p (a b c)")),
        deps=[aev[(hb2, NB - 1)] for hb2 in range(4)], inc=16, tag="st_a2")
    ag2 = A("gpsimd", lambda e: e.collective_compute(
        "AllGather", ALU.bypass, replica_groups=[list(range(R))],
        ins=[a2_in.ap().opt()], outs=[a2_out.ap().opt()]),
        deps=[st_a2], is_async=True, tag="ag2")
    ag2 = A("gpsimd", lambda e: e.collective_compute(
        "AllReduce", ALU.add, replica_groups=[list(range(R))],
        ins=[bar_in.ap().opt()], outs=[bar_out.ap().opt()]),
        deps=[ag2], is_async=True, tag="ag2bar")

    wo_ld = {}

    sreg2 = {}

    def mkreg2(e):
        r = e.alloc_register("oreg")
        e.reg_load(r, offs_sb[0:1, 24:25])
        sreg2["o"] = RuntimeValue(r, min_val=0, max_val=3584)
        return None
    st_reg2 = A("scalar", mkreg2, deps=[ld_c], inc=0, tag="oreg")

    a24 = a2_out.ap().rearrange("(rk p) (g c) -> rk p g c", p=P, g=4)

    def atldfn(e):
        for b in range(2):
            for rk in range(8):
                e.dma_start(
                    out=at_sb[:, b, rk * 2:rk * 2 + 2, :],
                    in_=a24[rk, :, b:b + 3:2, :][:, :, ds(sreg2["o"], 512)]
                    ).then_inc(sems["scalar"], 16)
    at_ld = A("scalar", atldfn, deps=[ag2, st_reg2, moe_hmm[-1]], inc=16 * 16, tag="atld",
              self_inc=True)

    if "ag7" in DEBUG_TAPS:
        def ag7fn(e):
            e.dma_start(out=taps["ag7"][:, 0:2048],
                        in_=a2_out.ap()[0:128, 0:2048]).then_inc(sems["sync"], 16)
            e.dma_start(out=taps["ag7"][:, 2048:4096],
                        in_=a2_in.ap()[0:128, 0:2048]).then_inc(sems["sync"], 16)
        A("sync", ag7fn, deps=[ag2], inc=32, tag="tap_ag7", self_inc=True)

    if "at7" in DEBUG_TAPS:
        A("sync", lambda e: e.dma_start(
            out=taps["at7"], in_=at_sb[:, :, :, :].rearrange("p a b c -> p (a b c)")),
            deps=[at_ld], inc=16, tag="tap_at7")

    op_mm, op_fin, o_st = {}, {}, {}
    xf_ld, g_st = {}, {}
    for ob in range(4):
        dwo = [asg[(3, NB - 1, NB - 1)]]
        if ob >= 2:
            dwo.append(op_mm[(ob - 2) * 8 + 7])
        wo_ld[ob] = A("gpsimd", (lambda ob: lambda e: e.dma_start(
            out=wob[:, ob % 2, :, :],
            in_=wo_d[ob].rearrange("p (hc f) -> p hc f", hc=DC)))(ob),
            deps=dwo, inc=16, tag=f"wo{ob}")
        for b in range(2):
            for ts in range(4):
                u = ob * 8 + b * 4 + ts
                slot = u % 4
                dop = [at_ld, wo_ld[ob], consts, ld_bo]
                if u >= 4:
                    dop.append(op_fin[u - 4])
                if u < 4:
                    dop.append(aexp[(3, NB - 1, NB - 1)][1])

                def opfn(b=b, ts=ts, ob=ob, slot=slot):
                    def f(e):
                        for h in range(NH):
                            MM(e, psA[:, slot, :], at_sb[:, b, h, ts * 128:ts * 128 + 128],
                               wob[:, ob % 2, h, :], h == 0, False)
                        return MM(e, psA[:, slot, :], onescol[:, :],
                                  bout_b[0:1, ob * 512:ob * 512 + 512], False, True)
                    return f
                op_mm[u] = A("tensor", opfn(), deps=dop, tag=f"op{u}")

                xd = x0_d if b == 0 else x1_d
                dx = [st_a2] if u < 2 else [o_st[u - 2]]
                if u < 2:
                    dx.append(aev[(3, NB - 1)])
                xf_ld[u] = A("sync", (lambda xd, ts, ob, u: lambda e: e.dma_start(
                    out=xrb[:, u % 2, :],
                    in_=xd[ts * 128:ts * 128 + 128, ob * 512:ob * 512 + 512]))(xd, ts, ob, u),
                    deps=dx, inc=16, tag=f"xf{u}")

                dg = [moe_yac[-1]]
                if u >= 2:
                    dg.append(op_fin[u - 2])
                else:
                    dg.append(asg[(3, NB - 1, NB - 1)])

                def gfn(b=b, ts=ts, ob=ob, u=u):
                    def f(e):
                        tc = b * 4 + ts
                        return e.activation(gsc[:, u % 2, :],
                                            y_acc[:, tc, ob * 512:ob * 512 + 512], AF.Silu)
                    return f
                g_st[u] = A("scalar", gfn(), deps=dg, tag=f"g{u}")

                def finfn(u=u, slot=slot):
                    def f(e):
                        e.tensor_tensor(sgs[:, u % 2, :], psA[:, slot, :], gsc[:, u % 2, :],
                                        ALU.mult)
                        e.drain()
                        ins = e.tensor_tensor(osb[:, u % 2, :], sgs[:, u % 2, :],
                                              xrb[:, u % 2, :], ALU.add)
                        e.drain()
                        return ins
                    return f
                op_fin[u] = A("vector", finfn(), deps=[op_mm[u], g_st[u], xf_ld[u]],
                              tag=f"fi{u}")

                o_st[u] = A("sync", (lambda b, ts, ob, u: lambda e: e.dma_start(
                    out=out_d[b, ts * 128:ts * 128 + 128, ob * 512:ob * 512 + 512],
                    in_=osb[:, u % 2, :]))(b, ts, ob, u),
                    deps=[op_fin[u]], inc=16, tag=f"os{u}")

    return _emit_all(nc, pl, sems, [o_st[31]])


def kernel(**inputs):
    in_maps = host_layouts(inputs)
    key = (PHASES, tuple(sorted(DEBUG_TAPS)))
    if key not in _CACHE:
        _CACHE[key] = build_program()
    nc = _CACHE[key]
    res = run_bass_kernel_spmd(nc, in_maps, core_ids=list(range(R)))
    out = np.zeros((2, 4096, D), np.float32)
    for r in range(R):
        o = np.asarray(res.results[r]["out"], np.float32)
        out[0, TB * r:TB * (r + 1), :] = o[0]
        out[1, TB * r:TB * (r + 1), :] = o[1]
    if DEBUG_TAPS:
        kernel.taps = [{k: np.asarray(v) for k, v in res.results[r].items()
                        if k.startswith("t_")} for r in range(R)]
    return out
